# version 27 (final)
# baseline (speedup 1.0000x reference)
import math

import numpy as np
import ml_dtypes

import concourse.bass as bass
import concourse.mybir as mybir
from concourse.bass_utils import run_bass_kernel_spmd

AF = mybir.ActivationFunctionType
ALU = mybir.AluOpType
F32 = mybir.dt.float32
BF16 = mybir.dt.bfloat16
BFNP = ml_dtypes.bfloat16

NCORES = 8
B, T, D = 8, 2048, 1024
NTB = 4
TBS = 512
HD = 64
NHA = 16
NHF = 8
WIN = 512
FINNER = 512
HID = 2730
FD1 = 2 * HID
EPS = 1e-5
DEPTH_F = 2

STRICT = False

OFF_Z = 0
OFF_ACT = 65568
OFF_QA = 98352
OFF_KA = 114736
OFF_BIG = 131120
OFF_PT = 163888
OFF_WR = 167984
OFF_SC = 172080
OFF_AUX = 178224
ARENA_BYTES = 206896



def _rotary_tables_f64():
    half = np.arange(0, HD, 2, dtype=np.float64)
    inv_freq = 1.0 / 10000.0 ** (half / HD)
    t = np.arange(T, dtype=np.float64)
    freqs = t[:, None] * inv_freq[None, :]
    freqs = np.concatenate([freqs, freqs], axis=-1)
    base = (half + 0.4 * HD) / (1.4 * HD)
    power = (t - T // 2) / float(WIN // 2)
    scale = base[None, :] ** power[:, None]
    scale = np.concatenate([scale, scale], axis=-1)
    cos, sin = np.cos(freqs), np.sin(freqs)
    sgn = np.where(np.arange(HD) < 32, -1.0, 1.0)[None, :]
    return ((cos * scale).T, (sgn * sin * scale).T,
            (cos / scale).T, (sgn * sin / scale).T)


def _hi_lo(a64):
    hi = a64.astype(np.float32)
    return hi, (a64 - hi.astype(np.float64)).astype(np.float32)


def _host_masks():
    p = np.arange(128)[:, None]
    f = np.arange(128)[None, :]
    out = np.zeros((128, 2, 128), dtype=np.float32)
    out[:, 0, :] = (f >= p).astype(np.float32)
    out[:, 1, :] = (f <= p).astype(np.float32)
    return out


def _w1_perm():
    perm, widths = [], []
    for m in range(21):
        perm.extend(range(128 * m, 128 * m + 128)); widths.append(128)
        perm.extend(range(HID + 128 * m, HID + 128 * m + 128)); widths.append(128)
    perm.extend(range(2688, HID)); widths.append(42)
    perm.extend(range(HID + 2688, FD1)); widths.append(42)
    return np.asarray(perm), widths


_W1PERM, _W1WIDTHS = _w1_perm()

SMALL_COLS = {}


def _build_small(inp):
    cols = []

    def add(name, arr2d):
        SMALL_COLS[name] = (len(cols), arr2d.shape[1])
        for i in range(arr2d.shape[1]):
            cols.append(arr2d[:, i])

    def tile_vec(v, width_list=None):
        if width_list is None:
            nt = (len(v) + 127) // 128
            width_list = [min(128, len(v) - 128 * j) for j in range(nt)]
        out = np.zeros((128, len(width_list)), dtype=np.float32)
        o = 0
        for j, w in enumerate(width_list):
            out[:w, j] = v[o:o + w]
            o += w
        return out

    for l in range(DEPTH_F):
        add(f"ln_g{l}", tile_vec(np.asarray(inp["L0_ffn_ln_g"][l], np.float32)))
        add(f"ln_b{l}", tile_vec(np.asarray(inp["L0_ffn_ln_b"][l], np.float32)))
        b1p = np.asarray(inp["L0_ffn_b1"][l], np.float32)[_W1PERM]
        add(f"b1{l}", tile_vec(b1p, _W1WIDTHS))
        add(f"b2{l}", tile_vec(np.asarray(inp["L0_ffn_b2"][l], np.float32)))
        qs = np.asarray(inp["L0_ffn_q_scale"][l], np.float32)
        ks = np.asarray(inp["L0_ffn_k_scale"][l], np.float32)
        add(f"qs{l}", np.concatenate([qs, qs])[:, None])
        add(f"qss{l}", np.concatenate([qs[32:], qs[:32], qs[32:], qs[:32]])[:, None])
        add(f"ks{l}", np.concatenate([ks, ks])[:, None])
        add(f"kss{l}", np.concatenate([ks[32:], ks[:32], ks[32:], ks[:32]])[:, None])
    return np.ascontiguousarray(np.stack(cols, axis=1).astype(np.float32))


def _host_prep(inp):
    h = {}
    h["wqt"] = np.ascontiguousarray(np.asarray(inp["L0_attn_w_qkv"], np.float32).T)
    h["wot"] = np.ascontiguousarray(np.asarray(inp["L0_attn_w_out"], np.float32).T)
    h["fqkv"] = np.ascontiguousarray(np.asarray(inp["L0_ffn_wqkv"], np.float32))
    h["fwo"] = np.ascontiguousarray(np.asarray(inp["L0_ffn_wo"], np.float32))
    h["fw1"] = np.ascontiguousarray(np.asarray(inp["L0_ffn_w1"], np.float32)[:, :, _W1PERM])
    h["fw2"] = np.ascontiguousarray(np.asarray(inp["L0_ffn_w2"], np.float32))
    h["wmod"] = np.ascontiguousarray(np.asarray(inp["L0_wmod"], np.float32))
    h["small"] = _build_small(inp)
    tabs = []
    for tbl in _rotary_tables_f64():
        hi, lo = _hi_lo(tbl)
        tabs.extend((hi, lo))
    h["rtab"] = np.ascontiguousarray(np.stack(tabs, axis=1).astype(np.float32))
    h["masks"] = _host_masks()
    return h



ENGS = ("tensor", "scalar", "vector", "gpsimd", "sync")


class Region:
    __slots__ = ("name", "lo", "hi", "w", "readers", "dead", "initdeps")

    def __init__(self, name, lo=-1, hi=-1):
        self.name = name
        self.lo, self.hi = lo, hi
        self.w = None
        self.readers = []
        self.dead = False
        self.initdeps = []

    def __repr__(self):
        return f"R({self.name})"


class Plan:
    def __init__(self, nc, block, strict=STRICT):
        self.nc = nc
        self.block = block
        self.strict = strict
        self.sems = {}
        self.cnt = {e: 0 for e in ENGS}
        self.seq = {e: 0 for e in ENGS}
        self.pending = {e: [] for e in ENGS}
        self.watermark = {e: {} for e in ENGS}
        self.incpts = {e: [] for e in ENGS}
        self.allwaits = []
        self.oplog = {e: [] for e in ENGS}
        self.nops = 0

    def _sem(self, name):
        if name not in self.sems:
            self.sems[name] = self.nc.semaphore(f"s_{name}").__enter__()
        return self.sems[name]

    def op(self, eng, fn, reads=(), writes=(), inc=False, dma=False, tag=""):
        deps = {}
        dseq = {}
        loose = []

        def add_dep(tok):
            if tok is None:
                return
            key, v, exact, was_dma, sq = tok
            if key == eng and key == "tensor" and not was_dma and not self.strict:
                return
            if was_dma and key == "gpsimd" and eng != "gpsimd":
                v += 16
            if not exact:
                loose.append((key, sq, tok))
                return
            if v > deps.get(key, 0):
                deps[key] = v
            dseq[key] = max(dseq.get(key, -1), sq)

        for r in reads:
            assert not r.dead, f"read of dead region {r.name} in {tag}"
            add_dep(r.w)
            for t in r.initdeps:
                add_dep(t)
        for r in writes:
            assert not r.dead, f"write of dead region {r.name} in {tag}"
            add_dep(r.w)
            for t in r.readers:
                add_dep(t)
            for t in r.initdeps:
                add_dep(t)

        for key, sq, t in loose:
            if dseq.get(key, -1) >= sq:
                continue
            import bisect
            pts = self.incpts[key]
            i = bisect.bisect_left(pts, (sq, -1))
            if i >= len(pts):
                raise RuntimeError(
                    f"dep on non-inc'ed {key} op (seq {sq}) needed by "
                    f"{eng}:{tag} with no covering inc — add inc=True")
            v = pts[i][1]
            if v > deps.get(key, 0):
                deps[key] = v
            dseq[key] = max(dseq.get(key, -1), pts[i][0])

        if deps.get("gpsimd", 0) > self.cnt["gpsimd"] and not (
                eng == "gpsimd" and dma):
            self.op("gpsimd", self.swdge_dummy, dma=True, tag="swdge_cover")
        if dma or self.strict:
            inc = True
        if inc:
            self.cnt[eng] += 16 if dma else 1
        self.seq[eng] += 1
        if inc:
            self.incpts[eng].append((self.seq[eng], self.cnt[eng]))
        tok = (eng, self.cnt[eng], inc, dma, self.seq[eng])
        waits = []
        for key, v in deps.items():
            if v > self.watermark[eng].get(key, 0):
                waits.append((self._sem(key), v))
                self.watermark[eng][key] = v
                self.allwaits.append((eng, key, v, tag))
        my_inc = (self._sem(eng), 16 if dma else 1) if inc else None
        self.nops += 1

        def emit(e, waits=waits, fn=fn, my_inc=my_inc):
            for s, v in waits:
                e.wait_ge(s, v)
            inst = fn(e)
            if my_inc is not None and inst is not None:
                inst.then_inc(my_inc[0], my_inc[1])

        self.pending[eng].append(emit)
        self.oplog[eng].append((dict(deps), (16 if dma else 1) if inc else 0, tag))
        for r in reads:
            r.readers.append(tok)
        for r in writes:
            r.w = tok
            r.readers = []
            r.initdeps = []
        return tok

    def flush(self):
        for e in ENGS:
            ops = self.pending[e]
            if not ops:
                continue
            self.pending[e] = []

            def body(engine, ops=ops):
                for f in ops:
                    f(engine)

            getattr(self.block, e)(body)


class Ctx:
    def __init__(self, nc, plan):
        self.nc = nc
        self.p = plan
        self.reg = {}
        self.ranged = []

    def R(self, name, takes=()):
        if name in self.reg:
            return self.reg[name]
        r = Region(name)
        for o in takes:
            o = self.reg[o] if isinstance(o, str) else o
            o.dead = True
            if o.w is not None:
                r.initdeps.append(o.w)
            r.initdeps.extend(o.readers)
        self.reg[name] = r
        return r

    def Ra(self, name, lo, size):
        r = self.reg.get(name)
        if r is not None and not r.dead:
            assert r.lo == lo and r.hi == lo + size, (name, lo, size, r.lo, r.hi)
            return r
        r = Region(name, lo, lo + size)
        for o in self.ranged:
            if o.hi > lo and o.lo < lo + size:
                if o.w is not None:
                    r.initdeps.append(o.w)
                r.initdeps.extend(o.readers)
                o.dead = True
        self.ranged.append(r)
        self.reg[name] = r
        return r

    def dma(self, queue, out_ap, in_ap, reads=(), writes=(), tag=""):
        return self.p.op(queue, lambda e: e.dma_start(out_ap, in_ap),
                         reads=reads, writes=writes, dma=True, tag=tag)

    def mm(self, out_ap, lhsT, rhs, start, stop, reads=(), writes=(),
           inc=False, tile_position=None, tag=""):
        return self.p.op(
            "tensor",
            lambda e: e.matmul(out_ap, lhsT, rhs, start=start, stop=stop,
                               tile_position=tile_position),
            reads=reads, writes=writes, inc=inc, tag=tag)

    def act(self, out_ap, in_ap, func, bias=0.0, scale=1.0, accum_out=None,
            reads=(), writes=(), inc=True, tag=""):
        return self.p.op(
            "scalar",
            lambda e: e.activation(out_ap, in_ap, func, bias=bias, scale=scale,
                                   accum_out=accum_out),
            reads=reads, writes=writes, inc=inc, tag=tag)

    def vop(self, fn, reads=(), writes=(), inc=True, eng="vector", tag=""):
        return self.p.op(eng, fn, reads=reads, writes=writes, inc=inc, tag=tag)



def build_program(dev_phase=99, dev_debug=None, strict=STRICT):
    dev_debug = dev_debug or {}
    nc = bass.Bass("TRN2", target_bir_lowering=False,
                   detect_race_conditions=False)

    xt = nc.dram_tensor("xt", [D, T], F32, kind="ExternalInput")
    wqt = nc.dram_tensor("wqt", [D, 3 * D], F32, kind="ExternalInput")
    wot = nc.dram_tensor("wot", [D, D], F32, kind="ExternalInput")
    fqkv = nc.dram_tensor("fqkv", [2, D, 3 * FINNER], F32, kind="ExternalInput")
    fwo = nc.dram_tensor("fwo", [2, FINNER, D], F32, kind="ExternalInput")
    fw1 = nc.dram_tensor("fw1", [2, D, FD1], F32, kind="ExternalInput")
    fw2 = nc.dram_tensor("fw2", [2, HID, D], F32, kind="ExternalInput")
    wmod = nc.dram_tensor("wmod", [D, 6 * D], F32, kind="ExternalInput")
    cvec = nc.dram_tensor("cvec", [128, 8], F32, kind="ExternalInput")
    NS = max(1, sum(v[1] for v in SMALL_COLS.values()))
    small = nc.dram_tensor("small", [128, NS], F32, kind="ExternalInput")
    rtab = nc.dram_tensor("rtab", [64, 8, T], F32, kind="ExternalInput")
    masks = nc.dram_tensor("masks", [128, 2, 128], F32, kind="ExternalInput")
    y = nc.dram_tensor("y", [D, T], F32, kind="ExternalOutput")
    xsc = nc.dram_tensor("xsc", [D, T], F32,
                         kind="ExternalOutput" if dev_debug else "Internal")
    dbgs = {}
    for name in dev_debug:
        dbgs[name] = nc.dram_tensor(f"dbg_{name}", [D, T], F32,
                                    kind="ExternalOutput")

    xt_k = xt.ap().rearrange("(k p) t -> k p t", p=128)
    xs_k = xsc.ap().rearrange("(k p) t -> k p t", p=128)
    y_k = y.ap().rearrange("(k p) t -> k p t", p=128)

    import contextlib
    with contextlib.ExitStack() as stack:
        arn = stack.enter_context(
            nc.sbuf_tensor("arn", [128, ARENA_BYTES // 2], BF16))
        sm = stack.enter_context(nc.sbuf_tensor("sm", [128, NS], F32))
        cvs = stack.enter_context(nc.sbuf_tensor("cvs", [128, 32], F32))
        cvb = stack.enter_context(nc.sbuf_tensor("cvb", [128, 8], BF16))
        modv = stack.enter_context(nc.sbuf_tensor("modv", [128, 80], F32))
        fci = stack.enter_context(nc.sbuf_tensor("fci", [128, 24], F32))
        ones = stack.enter_context(nc.sbuf_tensor("ones", [128, 128], BF16))
        bones = stack.enter_context(nc.sbuf_tensor("bones", [128, 128], BF16))
        mk = stack.enter_context(nc.sbuf_tensor("mk", [128, 256], BF16))
        ps = stack.enter_context(nc.psum_tensor("ps", [128, 4096], F32))
        block = stack.enter_context(nc.Block())

        plan = Plan(nc, block, strict=strict)
        plan.swdge_dummy = lambda e: e.dma_start(cvs[0:1, 30:32],
                                                 masks.ap()[0:1, 0, 0:2])
        C = Ctx(nc, plan)
        R = C.R

        def b16(off, words):
            assert off % 2 == 0
            return arn[:, off // 2: off // 2 + words]

        def f32(off, words):
            assert off % 4 == 0
            return arn[:, off // 2: off // 2 + 2 * words].bitcast(F32)

        def bank(i):
            return ps[:, 512 * i:512 * (i + 1)]

        psR = [R(f"ps{i}") for i in range(8)]

        fc_map = {}

        def fc(val):
            if val not in fc_map:
                i = len(fc_map)
                assert i < 24
                C.vop(lambda e, i=i, val=val: e.memset(fci[:, i:i + 1], val),
                      writes=[R("fconst")], tag=f"fc{val}")
                fc_map[val] = fci[:, i:i + 1]
            return fc_map[val]

        def smcol(name, j=0):
            c0, w = SMALL_COLS[name]
            assert j < w, (name, j, w)
            return sm[:, c0 + j:c0 + j + 1]

        def dump_fm(name, src_cb, nk=8):
            if name not in dbgs:
                return
            dk = dbgs[name].ap().rearrange("(k p) t -> k p t", p=128)
            for k in range(nk):
                ap, reads = src_cb(k)
                C.dma("gpsimd", dk[k], ap, reads=reads,
                      writes=[C.R(f"dbgo{name}")], tag=f"dbgd{name}")
            plan.flush()

        C.vop(lambda e: e.memset(ones[:, :], 1.0), writes=[R("ones")])
        C.vop(lambda e: e.memset(bones[:, :], 0.0), writes=[R("bones")])
        C.vop(lambda e: e.memset(bones[0:64, 0:64], 1.0), writes=[R("bones")])
        C.vop(lambda e: e.memset(bones[64:128, 64:128], 1.0), writes=[R("bones")])
        C.dma("sync", sm[:, :], small.ap(), writes=[R("sm")])
        C.dma("sync", cvs[:, 0:8], cvec.ap(), writes=[R("cv")])
        C.dma("gpsimd", mk[:, :].rearrange("p (a b) -> p a b", a=2), masks.ap(),
              writes=[R("mk")])
        C.act(cvs[:, 8:16], cvs[:, 0:8], AF.Silu, reads=[R("cv")], writes=[R("cvs")])
        C.vop(lambda e: e.tensor_copy(cvb[:, :], cvs[:, 8:16]),
              reads=[R("cvs")], writes=[R("cvb")])

        xfA = f32(OFF_AUX, 4096)
        xfB = f32(OFF_Z + 49152, 4096)
        xfR = [C.Ra("xf0", OFF_AUX, 16384), C.Ra("xf1", OFF_Z + 49152, 16384)]
        xbR = [C.Ra("xb0", OFF_Z + 24576, 8192), C.Ra("xb1", OFF_Z + 32768, 8192)]
        xqR = C.Ra("ln1xq", OFF_Z + 40960, 8192)
        h1R = [[C.Ra(f"h1_{k}_{tb}", OFF_ACT + 4096 * k + 1024 * tb, 1024)
                for tb in range(NTB)] for k in range(8)]
        ACTB = b16(OFF_ACT, 16392)

        def ln1_prefetch(tb):
            xfap = xfA if tb % 2 == 0 else xfB
            srcap = xt_k[:, :, TBS * tb:TBS * (tb + 1)].transpose([1, 0, 2])
            C.dma("sync", xfap.rearrange("p (k t) -> p k t", k=8), srcap,
                  writes=[xfR[tb % 2]], tag=f"ldx{tb}")
            C.vop(lambda e, tb=tb, xfap=xfap: e.tensor_copy(
                b16(OFF_Z + 24576 + 8192 * (tb % 2), 4096), xfap),
                reads=[xfR[tb % 2]], writes=[xbR[tb % 2]], tag=f"ldxb{tb}")

        ln1_prefetch(0)
        ln1_prefetch(1)

        wmod_k = wmod.ap().rearrange("(k p) f -> k p f", p=128)
        def wrRa(slot):
            if slot < 2:
                return C.Ra(f"wr{slot}", OFF_WR + 2048 * slot, 2048)
            if slot < 4:
                return C.Ra(f"wr{slot}", OFF_PT + 2048 * (slot - 2), 2048)
            if slot < 7:
                return C.Ra(f"wr{slot}", OFF_SC + 2048 * (slot - 4), 2048)
            return C.Ra(f"wr{slot}", OFF_AUX + 24576 + 2048 * (slot - 7), 2048)

        def wr_slot(slot):
            if slot < 2:
                return b16(OFF_WR + 2048 * slot, 1024)
            if slot < 4:
                return b16(OFF_PT + 2048 * (slot - 2), 1024)
            if slot < 7:
                return b16(OFF_SC + 2048 * (slot - 4), 1024)
            return b16(OFF_AUX + 24576 + 2048 * (slot - 7), 1024)

        for j in range(48):
            slot = j % 2
            wsl = wr_slot(slot)
            C.dma("gpsimd", wsl.rearrange("p (k c) -> p k c", k=8),
                  wmod_k[:, :, 128 * j:128 * (j + 1)].transpose([1, 0, 2]),
                  writes=[wrRa(slot)], tag=f"wmod{j}")
            for k in range(8):
                C.mm(ps[:, 3072 + j:3073 + j], wsl[:, 128 * k:128 * (k + 1)],
                     cvb[:, k:k + 1], start=(k == 0), stop=(k == 7),
                     reads=[wrRa(slot), R("cvb")], writes=[psR[6]],
                     inc=(k == 7), tag=f"mod{j}")
        C.vop(lambda e: e.tensor_copy(modv[:, 0:48], ps[:, 3072:3120]),
              reads=[psR[6]], writes=[R("modv")], tag="modevac")
        C.vop(lambda e: e.tensor_scalar(modv[:, 48:56], modv[:, 8:16], 1.0, None, ALU.add),
              reads=[R("modv")], writes=[R("modd")])
        C.vop(lambda e: e.tensor_scalar(modv[:, 56:64], modv[:, 32:40], 1.0, None, ALU.add),
              reads=[R("modv")], writes=[R("modd")])
        for (src, dst) in ((16, 64), (40, 72)):
            C.vop(lambda e, s=src: e.tensor_scalar(
                cvs[:, 16:24], modv[:, s:s + 8], 0.0, 2.0, ALU.is_ge, ALU.mult),
                reads=[R("modv")], writes=[R("sgn")])
            C.vop(lambda e: e.tensor_scalar(
                cvs[:, 16:24], cvs[:, 16:24], -1.0, None, ALU.add),
                reads=[R("sgn")], writes=[R("sgn")])
            C.vop(lambda e, s=src: e.tensor_scalar(
                cvs[:, 24:32], modv[:, s:s + 8], -1.0, None, ALU.mult),
                reads=[R("modv")], writes=[R("mag")])
            C.vop(lambda e, s=src: e.tensor_max(
                cvs[:, 24:32], cvs[:, 24:32], modv[:, s:s + 8]),
                reads=[R("modv"), R("mag")], writes=[R("mag")])
            C.vop(lambda e: e.tensor_scalar(
                cvs[:, 24:32], cvs[:, 24:32], 0.25, None, ALU.max),
                reads=[R("mag")], writes=[R("mag")])
            C.vop(lambda e, d=dst: e.tensor_mul(
                modv[:, d:d + 8], cvs[:, 16:24], cvs[:, 24:32]),
                reads=[R("sgn"), R("mag")], writes=[R("modd")])
        plan.flush()

        def layernorm(f32_src, bf_src, xq_ap, xqR, scale_col, bias_col,
                      out_dst, tag, prefetch=None, post_tb=None):
            if prefetch is not None:
                prefetch(0)
                if NTB > 1:
                    prefetch(1)
            for tb in range(NTB):
                par = tb % 2
                pb = par * 2
                mu = f32(OFF_AUX + 16384 + 4096 * par, 512)
                varb = f32(OFF_AUX + 16384 + 4096 * par + 2048, 512)
                muR = C.Ra(tag + f"mu{par}", OFF_AUX + 16384 + 4096 * par, 2048)
                vaR = C.Ra(tag + f"va{par}", OFF_AUX + 18432 + 4096 * par, 2048)
                bfap, bfreads = bf_src(tb)
                C.act(xq_ap, bfap, AF.Square, reads=bfreads,
                      writes=[xqR], tag=f"{tag}sq{tb}")
                for k in range(8):
                    C.mm(bank(pb), ones[:, :], bfap[:, 512 * k:512 * (k + 1)],
                         start=(k == 0), stop=(k == 7),
                         reads=list(bfreads) + [R("ones")], writes=[psR[pb]],
                         inc=(k == 7), tag=f"{tag}sum{tb}")
                    C.mm(bank(pb + 1), ones[:, :], xq_ap[:, 512 * k:512 * (k + 1)],
                         start=(k == 0), stop=(k == 7),
                         reads=[xqR, R("ones")], writes=[psR[pb + 1]],
                         inc=(k == 7), tag=f"{tag}ssq{tb}")
                C.vop(lambda e, mu=mu, pb=pb: e.tensor_scalar(
                    mu, bank(pb), 1.0 / D, None, ALU.mult),
                    reads=[psR[pb]], writes=[muR], tag=f"{tag}mu{tb}")
                C.vop(lambda e, mu=mu, varb=varb: e.tensor_mul(varb, mu, mu),
                      reads=[muR], writes=[vaR], tag=f"{tag}mu2{tb}")
                C.vop(lambda e, varb=varb, pb=pb: e.scalar_tensor_tensor(
                    varb, bank(pb + 1), 1.0 / D, varb, ALU.mult, ALU.subtract),
                    reads=[psR[pb + 1], vaR], writes=[vaR], tag=f"{tag}var{tb}")
                C.act(varb, varb, AF.Ln, bias=fc(EPS),
                      reads=[vaR, R("fconst")], writes=[vaR],
                      tag=f"{tag}ln{tb}")
                C.act(varb, varb, AF.Exp, scale=fc(-0.5),
                      reads=[vaR, R("fconst")], writes=[vaR],
                      tag=f"{tag}exp{tb}")
                C.vop(lambda e, mu=mu, varb=varb: e.tensor_mul(mu, mu, varb),
                      reads=[vaR], writes=[muR], tag=f"{tag}b3{tb}")
                for k in range(8):
                    sl = 2 * par + (k % 2)
                    off1 = (OFF_PT + 2048 * sl) if sl < 2 else \
                        (OFF_AUX + 24576 + 2048 * (sl - 2))
                    m1 = f32(off1, 512)
                    m1R = C.Ra(tag + f"m1_{sl}", off1, 2048)
                    fap, freads = f32_src(k, tb)
                    C.vop(lambda e, m1=m1, fap=fap, varb=varb: e.tensor_mul(
                        m1, fap, varb),
                        reads=list(freads) + [vaR], writes=[m1R],
                        tag=f"{tag}a1_{tb}_{k}")
                    C.vop(lambda e, m1=m1, mu=mu: e.tensor_sub(m1, m1, mu),
                          reads=[muR], writes=[m1R], tag=f"{tag}a2_{tb}_{k}")
                    oap, owrites, oreads = out_dst(k, tb)
                    C.act(oap, m1, AF.Identity, scale=scale_col(k), bias=bias_col(k),
                          reads=[m1R, R("modd"), R("sm")] + list(oreads),
                          writes=owrites, tag=f"{tag}a3_{tb}_{k}")
                if post_tb is not None:
                    post_tb(tb)
                if prefetch is not None and tb + 2 < NTB:
                    prefetch(tb + 2)

        layernorm(
            f32_src=lambda k, tb: ((xfA if tb % 2 == 0 else xfB)
                                   [:, 512 * k:512 * (k + 1)], [xfR[tb % 2]]),
            bf_src=lambda tb: (b16(OFF_Z + 24576 + 8192 * (tb % 2), 4096),
                               [xbR[tb % 2]]),
            xq_ap=b16(OFF_Z + 40960, 4096), xqR=xqR,
            scale_col=lambda k: modv[:, 48 + k:49 + k],
            bias_col=lambda k: modv[:, k:k + 1],
            out_dst=lambda k, tb: (ACTB[:, 2048 * k + 512 * tb:
                                        2048 * k + 512 * tb + 512],
                                   [h1R[k][tb]], []),
            tag="ln1", prefetch=lambda tb: (ln1_prefetch(tb)
                                            if tb >= 2 else None))
        plan.flush()

        dump_fm("h1", lambda k: (ACTB[:, 2048 * k:2048 * (k + 1)], h1R[k]))

        if dev_phase < 3:
            early_exit()
            return nc

        OFF_DV = OFF_Z
        OFF_DO = OFF_Z + 24576
        dvR = C.Ra("dv", OFF_Z, 24576)
        doR = [C.Ra(f"do{gp}", OFF_Z + 24576 + 4096 * gp, 4096)
               for gp in range(8)]
        DEN = f32(OFF_AUX, 2048)
        DENS = f32(OFF_AUX + 8192, 2048)
        wqt_k = wqt.ap().rearrange("(k p) f -> k p f", p=128)

        def ov_pair(gp):
            return b16(OFF_DO + 4096 * gp, 2048)

        def qa_pair(pl):
            return b16(OFF_QA + 4096 * pl, 2048)

        def ka_pair(pl):
            return b16(OFF_KA + 4096 * pl, 2048)

        def v_lhsT(kt, pl, odd):
            return b16(OFF_DV + 2 * (kt * 768 + pl * 192 + (64 if odd else 0)), 128)

        qaR = [C.Ra(f"qa{pl}", OFF_QA + 4096 * pl, 4096) for pl in range(4)]
        kaR = [C.Ra(f"ka{pl}", OFF_KA + 4096 * pl, 4096) for pl in range(4)]

        def ptRa(i):
            return C.Ra(f"pt{i}", OFF_PT + 2048 * i, 2048)

        vones = b16(OFF_DV, 12288).rearrange("p (a c) -> p a c", c=192)[:, :, 64:128]
        C.vop(lambda e: e.memset(vones, 1.0), writes=[dvR], tag="vones")

        def stream_w_tile(src_k, col0, slot, tagp, nk=8, width=128):
            wsl = wr_slot(slot)
            C.dma("gpsimd",
                  wsl[:, 0:nk * width].rearrange("p (k c) -> p k c", k=nk),
                  src_k[:, :, col0:col0 + width].transpose([1, 0, 2]),
                  writes=[wrRa(slot)], tag=tagp)
            return wsl

        sl_ctr = [1]

        def next_slot():
            sl_ctr[0] ^= 1
            return sl_ctr[0]

        def emit_qk_tile(w, qk, pl):
            jf = 1024 * qk + 128 * (4 * w + pl)
            slot = next_slot()
            wsl = stream_w_tile(wqt_k, jf, slot, f"wq{w}{qk}{pl}")
            dstR = (qaR if qk == 0 else kaR)[pl]
            dst = (qa_pair if qk == 0 else ka_pair)(pl)
            for tb in range(NTB):
                pb = tb % 2
                for kt in range(8):
                    C.mm(bank(pb), wsl[:, 128 * kt:128 * (kt + 1)],
                         ACTB[:, 2048 * kt + 512 * tb:2048 * kt + 512 * tb + 512],
                         start=(kt == 0), stop=(kt == 7),
                         reads=[wrRa(slot), h1R[kt]],
                         writes=[psR[pb]], inc=(kt == 7),
                         tag=f"qk{w}{qk}{pl}{tb}")
                if qk == 0:
                    C.vop(lambda e, dst=dst, tb=tb, pb=pb: e.tensor_scalar(
                        dst[:, 512 * tb:512 * (tb + 1)], bank(pb),
                        0.125, None, ALU.mult),
                        reads=[psR[pb]], writes=[dstR],
                        tag=f"qkev{w}{qk}{pl}{tb}")
                else:
                    C.vop(lambda e, dst=dst, tb=tb, pb=pb: e.tensor_copy(
                        dst[:, 512 * tb:512 * (tb + 1)], bank(pb)),
                        reads=[psR[pb]], writes=[dstR],
                        tag=f"qkev{w}{qk}{pl}{tb}")

        def emit_v_tile(w, jv, vbase, vR):
            jf = 2048 + 512 * w + 128 * jv
            slot = next_slot()
            wsl = stream_w_tile(wqt_k, jf, slot, f"wv{w}{jv}")
            for tt in range(16):
                pb = tt % 2
                for kt in range(8):
                    C.mm(bank(pb)[:, 0:128],
                         ACTB[:, 2048 * kt + 128 * tt:2048 * kt + 128 * tt + 128],
                         wsl[:, 128 * kt:128 * (kt + 1)],
                         start=(kt == 0), stop=(kt == 7),
                         reads=[wrRa(slot), h1R[kt]],
                         writes=[psR[pb]], inc=(kt == 7),
                         tag=f"v{w}{jv}{tt}")
                for odd in range(2):
                    dstv = b16(vbase + 2 * (tt * 768 + jv * 192 + 128 * odd), 64)
                    C.vop(lambda e, dstv=dstv, pb=pb, odd=odd: e.tensor_copy(
                        dstv, bank(pb)[:, 64 * odd:64 * odd + 64]),
                        reads=[psR[pb]], writes=[vR], tag=f"vev{w}{jv}{tt}")


        def dense_attention_pair(chains, vbase, vR):
            tmpR = C.Ra("dentmp", OFF_AUX + 16384, 8192)
            TMP = f32(OFF_AUX + 16384, 2048)
            denRs = {}
            for ci, (pl, gp, bm) in enumerate(chains):
                denRs[ci] = C.Ra(f"den{ci}", bm["den"], 8192)

            def s_banks(bm, kt):
                return bm["s"][kt % 2]

            def pt_ap(bm, kt, words=1024):
                return b16(bm["ptb"] + 2048 * (kt % 2), words)

            def ptR(bm, kt):
                return C.Ra(f"pt{bm['ptb']}_{kt % 2}", bm["ptb"] + 2048 * (kt % 2), 2048)

            for qb in range(NTB):
                nkt = 4 * qb + 4
                for step in range(nkt + 1):
                    for ci, (pl, gp, bm) in enumerate(chains):
                        if step > 0:
                            kt = step - 1
                            d0 = max(0, 128 * kt - 512 * qb)
                            p = kt % 2
                            ptb = OFF_PT + 4096 * p + 2048 * ci
                            ptuR = C.Ra(f"ptu{p}_{ci}", ptb, 2048)
                            ptap = b16(ptb, 1024)
                            b0, b1 = s_banks(bm, kt)
                            src3 = ps[:, 512 * b0:512 * b0 + 1024].rearrange(
                                "p (a c) -> p a c", c=512)[:, :, d0:512]
                            dst3 = b16(ptb, 1024).rearrange(
                                "p (a c) -> p a c", c=512)[:, :, d0:512]
                            C.act(dst3, src3, AF.Exp,
                                  reads=[psR[b0], psR[b1]],
                                  writes=[ptuR], tag=f"exp{gp}_{qb}_{kt}")
                            if 128 * kt - 512 * qb >= 0:
                                for odd in range(2):
                                    sl = ptap[:, 512 * odd + d0:512 * odd + d0 + 128]
                                    C.vop(lambda e, sl=sl: e.tensor_mul(
                                        sl, sl, mk[:, 0:128]),
                                        reads=[R("mk")], writes=[ptuR],
                                        tag=f"msk{gp}_{qb}_{kt}")
                            for odd in range(2):
                                C.mm(ps[:, 512 * (bm["pv"] + odd) + d0:
                                        512 * (bm["pv"] + odd) + 512],
                                     b16(vbase + 2 * (kt * 768 + pl * 192 +
                                                      (64 if odd else 0)), 128),
                                     b16(ptb + 2 * (512 * odd + d0), 512 - d0),
                                     start=(kt == 0), stop=(kt == nkt - 1),
                                     reads=[ptuR, vR],
                                     writes=[psR[bm["pv"] + odd]],
                                     inc=(kt == nkt - 1),
                                     tag=f"pv{gp}_{qb}_{kt}")
                        if step < nkt:
                            kt = step
                            b0, b1 = s_banks(bm, kt)
                            for odd in range(2):
                                r0 = 64 * odd
                                C.mm(bank(b1 if odd else b0),
                                     ka_pair(pl)[r0:r0 + 64, 128 * kt:128 * (kt + 1)],
                                     qa_pair(pl)[r0:r0 + 64, 512 * qb:512 * (qb + 1)],
                                     start=True, stop=True,
                                     reads=[qaR[pl], kaR[pl]],
                                     writes=[psR[b1 if odd else b0]],
                                     inc=(odd == 1), tile_position=(r0, 0),
                                     tag=f"st{gp}_{qb}_{kt}")
                    if step == nkt:
                        for ci, (pl, gp, bm) in enumerate(chains):
                            denR = denRs[ci]
                            DENc = f32(bm["den"], 2048)
                            ot = ov_pair(gp)
                            pv = bm["pv"]
                            C.vop(lambda e, ot=ot, qb=qb, pv=pv: e.tensor_copy(
                                ot[0:64, 512 * qb:512 * (qb + 1)], bank(pv)[0:64, :]),
                                reads=[psR[pv]], writes=[doR[gp]], tag=f"oev{gp}_{qb}")
                            C.vop(lambda e, ot=ot, qb=qb, pv=pv: e.tensor_copy(
                                ot[64:128, 512 * qb:512 * (qb + 1)],
                                bank(pv + 1)[64:128, :]),
                                reads=[psR[pv + 1]], writes=[doR[gp]],
                                tag=f"oev2{gp}_{qb}")
                            C.vop(lambda e, qb=qb, pv=pv, DENc=DENc: e.tensor_copy(
                                DENc[64:128, 512 * qb:512 * (qb + 1)],
                                bank(pv)[64:128, :]),
                                reads=[psR[pv]], writes=[denR], tag=f"dev{gp}_{qb}")
                            C.vop(lambda e, qb=qb, pv=pv, DENc=DENc: e.tensor_copy(
                                DENc[0:64, 512 * qb:512 * (qb + 1)],
                                bank(pv + 1)[0:64, :]),
                                reads=[psR[pv + 1]], writes=[denR],
                                tag=f"dev2{gp}_{qb}")
            for ci, (pl, gp, bm) in enumerate(chains):
                denR = denRs[ci]
                DENc = f32(bm["den"], 2048)
                C.dma("sync", TMP[0:64, :], DENc[0:64, :],
                      reads=[denR], writes=[tmpR], tag=f"dsw{gp}")
                C.dma("sync", DENc[0:64, :], DENc[64:128, :],
                      reads=[denR, tmpR], writes=[denR], tag=f"dsw2{gp}")
                C.dma("sync", DENc[64:128, :], TMP[0:64, :],
                      reads=[tmpR], writes=[denR], tag=f"dsw3{gp}")
                C.act(DENc, DENc, AF.Ln, reads=[denR], writes=[denR],
                      tag=f"dln{gp}")
                C.act(DENc, DENc, AF.Exp, scale=fc(-1.0),
                      reads=[denR, R("fconst")], writes=[denR], tag=f"dex{gp}")
                C.vop(lambda e, gp=gp, DENc=DENc: e.tensor_mul(
                    ov_pair(gp), ov_pair(gp), DENc),
                    reads=[denR], writes=[doR[gp]], tag=f"onorm{gp}")

        BM_A = dict(s=((2, 3), (2, 3)), pv=6, ptb=OFF_PT, den=OFF_AUX)
        BM_B = dict(s=((4, 5), (4, 5)), pv=0, ptb=OFF_WR, den=OFF_AUX + 8192)
        w0R = [C.Ra(f"dw0_{j}", OFF_BIG + 2048 * j, 2048) for j in range(12)]
        for sj, jf in enumerate([128 * j for j in range(4)] +
                                [1024 + 128 * j for j in range(4)] +
                                [2048 + 128 * j for j in range(4)]):
            wsl = b16(OFF_BIG + 2048 * sj, 1024)
            C.dma("gpsimd", wsl.rearrange("p (k c) -> p k c", k=8),
                  wqt_k[:, :, jf:jf + 128].transpose([1, 0, 2]),
                  writes=[w0R[sj]], tag=f"dw0_{sj}")

        for tb in range(NTB):
            for qk in range(2):
                for pl in range(4):
                    sj = 4 * qk + pl
                    wsl = b16(OFF_BIG + 2048 * sj, 1024)
                    dstR = (qaR if qk == 0 else kaR)[pl]
                    dst = (qa_pair if qk == 0 else ka_pair)(pl)
                    pb = sj % 2
                    for kt in range(8):
                        C.mm(bank(pb), wsl[:, 128 * kt:128 * (kt + 1)],
                             ACTB[:, 2048 * kt + 512 * tb:2048 * kt + 512 * tb + 512],
                             start=(kt == 0), stop=(kt == 7),
                             reads=[w0R[sj], h1R[kt][tb]],
                             writes=[psR[pb]], inc=(kt == 7),
                             tag=f"qk0{qk}{pl}{tb}")
                    if qk == 0:
                        C.vop(lambda e, dst=dst, tb=tb, pb=pb: e.tensor_scalar(
                            dst[:, 512 * tb:512 * (tb + 1)], bank(pb),
                            0.125, None, ALU.mult),
                            reads=[psR[pb]], writes=[dstR],
                            tag=f"qkev0{qk}{pl}{tb}")
                    else:
                        C.vop(lambda e, dst=dst, tb=tb, pb=pb: e.tensor_copy(
                            dst[:, 512 * tb:512 * (tb + 1)], bank(pb)),
                            reads=[psR[pb]], writes=[dstR],
                            tag=f"qkev0{qk}{pl}{tb}")
        for jv in range(4):
            wsl = b16(OFF_BIG + 2048 * (8 + jv), 1024)
            for tt in range(16):
                pb = tt % 2
                for kt in range(8):
                    C.mm(bank(pb)[:, 0:128],
                         ACTB[:, 2048 * kt + 128 * tt:2048 * kt + 128 * tt + 128],
                         wsl[:, 128 * kt:128 * (kt + 1)],
                         start=(kt == 0), stop=(kt == 7),
                         reads=[w0R[8 + jv], h1R[kt][tt // 4]],
                         writes=[psR[pb]], inc=(kt == 7),
                         tag=f"v0{jv}{tt}")
                for odd in range(2):
                    dstv = b16(OFF_DV + 2 * (tt * 768 + jv * 192 + 128 * odd), 64)
                    C.vop(lambda e, dstv=dstv, pb=pb, odd=odd: e.tensor_copy(
                        dstv, bank(pb)[:, 64 * odd:64 * odd + 64]),
                        reads=[psR[pb]], writes=[dvR], tag=f"vev0{jv}{tt}")

        def w1_qk_mm(pl, slot_q, slot_k):
            for qk, slot in ((0, slot_q), (1, slot_k)):
                wsl = wr_slot(slot)
                dstR = (qaR if qk == 0 else kaR)[pl]
                dst = (qa_pair if qk == 0 else ka_pair)(pl)
                for tb in range(NTB):
                    pb = tb % 2
                    for kt in range(8):
                        C.mm(bank(pb), wsl[:, 128 * kt:128 * (kt + 1)],
                             ACTB[:, 2048 * kt + 512 * tb:
                                  2048 * kt + 512 * tb + 512],
                             start=(kt == 0), stop=(kt == 7),
                             reads=[wrRa(slot), h1R[kt][tb]],
                             writes=[psR[pb]], inc=(kt == 7),
                             tag=f"qk1{qk}{pl}{tb}")
                    if qk == 0:
                        C.vop(lambda e, dst=dst, tb=tb, pb=pb: e.tensor_scalar(
                            dst[:, 512 * tb:512 * (tb + 1)], bank(pb),
                            0.125, None, ALU.mult),
                            reads=[psR[pb]], writes=[dstR],
                            tag=f"qkev1{qk}{pl}{tb}")
                    else:
                        C.vop(lambda e, dst=dst, tb=tb, pb=pb: e.tensor_copy(
                            dst[:, 512 * tb:512 * (tb + 1)], bank(pb)),
                            reads=[psR[pb]], writes=[dstR],
                            tag=f"qkev1{qk}{pl}{tb}")

        def w1_v_mm(jv, vR):
            wsl = b16(OFF_BIG + 2048 * (12 + jv), 1024)
            for tt in range(16):
                pb = tt % 2
                for kt in range(8):
                    C.mm(bank(pb)[:, 0:128],
                         ACTB[:, 2048 * kt + 128 * tt:2048 * kt + 128 * tt + 128],
                         wsl[:, 128 * kt:128 * (kt + 1)],
                         start=(kt == 0), stop=(kt == 7),
                         reads=[C.Ra(f"dw1v{jv}", OFF_BIG + 2048 * (12 + jv), 2048),
                                h1R[kt][tt // 4]],
                         writes=[psR[pb]], inc=(kt == 7),
                         tag=f"v1{jv}{tt}")
                for odd in range(2):
                    dstv = b16(OFF_BIG + 2 * (tt * 768 + jv * 192 + 128 * odd), 64)
                    C.vop(lambda e, dstv=dstv, pb=pb, odd=odd: e.tensor_copy(
                        dstv, bank(pb)[:, 64 * odd:64 * odd + 64]),
                        reads=[psR[pb]], writes=[vR], tag=f"vev1{jv}{tt}")

        dv1R = None
        slot_rot = [0]
        for blk in range(2):
            for pl in (2 * blk, 2 * blk + 1):
                slot_q = 4 + (slot_rot[0] % 5); slot_rot[0] += 1
                slot_k = 4 + (slot_rot[0] % 5); slot_rot[0] += 1
                stream_w_tile(wqt_k, 128 * (4 + pl), slot_q, f"dw1q{pl}")
                stream_w_tile(wqt_k, 1024 + 128 * (4 + pl), slot_k, f"dw1k{pl}")
                C.dma("gpsimd",
                      b16(OFF_BIG + 2048 * (12 + pl), 1024)
                      .rearrange("p (k c) -> p k c", k=8),
                      wqt_k[:, :, 2048 + 512 + 128 * pl:
                            2048 + 512 + 128 * pl + 128].transpose([1, 0, 2]),
                      writes=[C.Ra(f"dw1v{pl}", OFF_BIG + 2048 * (12 + pl), 2048)],
                      tag=f"dw1v{pl}")
                if pl == 2 * blk:
                    sq0, sk0 = slot_q, slot_k
                else:
                    sq1, sk1 = slot_q, slot_k
            dense_attention_pair(
                [(2 * blk, 2 * blk, BM_A), (2 * blk + 1, 2 * blk + 1, BM_B)],
                OFF_DV, dvR)
            w1_qk_mm(2 * blk, sq0, sk0)
            w1_qk_mm(2 * blk + 1, sq1, sk1)
            if blk == 0:
                dv1R = C.Ra("dv1", OFF_BIG, 24576)
                vones1 = b16(OFF_BIG, 12288).rearrange(
                    "p (a c) -> p a c", c=192)[:, :, 64:128]
                C.vop(lambda e, v=vones1: e.memset(v, 1.0), writes=[dv1R],
                      tag="vones1")
            w1_v_mm(2 * blk, dv1R)
            w1_v_mm(2 * blk + 1, dv1R)
        for blk in range(2):
            dense_attention_pair(
                [(2 * blk, 4 + 2 * blk, BM_A), (2 * blk + 1, 5 + 2 * blk, BM_B)],
                OFF_BIG, dv1R)

        plan.flush()
        dump_fm("o", lambda k: (ov_pair(k), [doR[k]]))
        if dev_phase < 4:
            early_exit()
            return nc

        wot_k = wot.ap().rearrange("(k p) f -> k p f", p=128)
        XS = f32(OFF_AUX, 4096)
        xsR = C.Ra("xs", OFF_AUX, 16384)
        xscR = C.R("xsc")
        XOp = [f32(OFF_BIG + 16384, 4096), f32(OFF_ACT + 16384, 4096)]
        xoRp = [C.Ra("xo0", OFF_BIG + 16384, 16384),
                C.Ra("xo1", OFF_ACT + 16384, 16384)]
        wouR = [C.Ra(f"wou{j}", OFF_ACT + 2048 * j, 2048) for j in range(8)]
        for j in range(8):
            wsl = b16(OFF_ACT + 2048 * j, 1024)
            C.dma("gpsimd", wsl.rearrange("p (k c) -> p k c", k=8),
                  wot_k[:, :, 128 * j:128 * (j + 1)].transpose([1, 0, 2]),
                  writes=[wouR[j]], tag=f"wo{j}")
        def wout_tb(tb):
            XO = XOp[tb % 2]
            C.dma("gpsimd", XS.rearrange("p (k t) -> p k t", k=8),
                  xt_k[:, :, TBS * tb:TBS * (tb + 1)].transpose([1, 0, 2]),
                  writes=[xsR], tag=f"woldx{tb}")
            for j in range(8):
                wsl = b16(OFF_ACT + 2048 * j, 1024)
                pb = j % 2
                for kt in range(8):
                    C.mm(bank(pb), wsl[:, 128 * kt:128 * (kt + 1)],
                         ov_pair(kt)[:, 512 * tb:512 * (tb + 1)],
                         start=(kt == 0), stop=(kt == 7),
                         reads=[wouR[j], doR[kt]], writes=[psR[pb]],
                         inc=(kt == 7), tag=f"wo{j}{tb}")
                C.vop(lambda e, j=j, pb=pb, XO=XO: e.scalar_tensor_tensor(
                    XO[:, 512 * j:512 * (j + 1)], bank(pb), modv[:, 64 + j:65 + j],
                    XS[:, 512 * j:512 * (j + 1)], ALU.mult, ALU.add),
                    reads=[psR[pb], R("modd"), xsR], writes=[xoRp[tb % 2]],
                    tag=f"wores{j}{tb}")
            C.dma("sync", xs_k[:, :, TBS * tb:TBS * (tb + 1)].transpose([1, 0, 2]),
                  XO.rearrange("p (k t) -> p k t", k=8),
                  reads=[xoRp[tb % 2]], writes=[xscR], tag=f"spill{tb}")

        for tb in range(NTB):
            wout_tb(tb)
        if dev_phase < 5:
            early_exit()
            return nc

        zR = [C.Ra(f"z{k}", OFF_Z + 4 * 2049 * k, 8196) for k in range(8)]
        zbTR = [C.R(f"zbt{tb}") for tb in range(NTB)]
        C.zbTR = zbTR
        ZB = b16(OFF_ACT, 16392)
        zv = f32(OFF_Z, 16392).rearrange("p (k t) -> p k t", t=T + 1)
        zbv = ZB.rearrange("p (k t) -> p k t", t=T + 1)
        zgR = C.R("zgv")
        C.vop(lambda e: e.memset(zv[:, :, 0:1], 0.0),
              reads=[], writes=[zgR] + zR, tag="zguard")
        C.vop(lambda e: e.memset(zbv[:, :, 0:1], 0.0),
              reads=[], writes=[zgR] + zbTR + wouR + [xoRp[1]],
              tag="zbguard")
        ln2xb = [C.Ra("ln2xb0", OFF_QA, 8192), C.Ra("ln2xb1", OFF_QA + 8192, 8192)]
        ln2xqR = C.Ra("ln2xq", OFF_KA, 8192)
        ln2fR = [C.Ra("ln2f0", OFF_AUX, 16384), C.Ra("ln2f1", OFF_BIG, 16384)]
        xfA2 = f32(OFF_AUX, 4096)
        xfB2 = f32(OFF_BIG, 4096)

        def z_tile(k, tb):
            return f32(OFF_Z + 4 * (2049 * k + 1 + 512 * tb), 512)

        def ln2_prefetch(tb):
            xfap = xfA2 if tb % 2 == 0 else xfB2
            srcap = xs_k[:, :, TBS * tb:TBS * (tb + 1)].transpose([1, 0, 2])
            C.dma("gpsimd" if tb % 2 == 0 else "sync",
                  xfap.rearrange("p (k t) -> p k t", k=8), srcap,
                  reads=[xscR], writes=[ln2fR[tb % 2]], tag=f"l2ldx{tb}")
            C.vop(lambda e, tb=tb, xfap=xfap: e.tensor_copy(
                b16(OFF_QA + 8192 * (tb % 2), 4096), xfap),
                reads=[ln2fR[tb % 2]], writes=[ln2xb[tb % 2]],
                tag=f"l2cast{tb}")

        layernorm(
            f32_src=lambda k, tb: ((xfA2 if tb % 2 == 0 else xfB2)
                                   [:, 512 * k:512 * (k + 1)],
                                   [ln2fR[tb % 2]]),
            bf_src=lambda tb: (b16(OFF_QA + 8192 * (tb % 2), 4096),
                               [ln2xb[tb % 2]]),
            xq_ap=b16(OFF_KA, 4096), xqR=ln2xqR,
            scale_col=lambda k: modv[:, 56 + k:57 + k],
            bias_col=lambda k: modv[:, 24 + k:25 + k],
            out_dst=lambda k, tb: (z_tile(k, tb), [zR[k]], [zgR]),
            tag="ln2", prefetch=ln2_prefetch,
            post_tb=lambda tb: C.vop(lambda e, tb=tb: e.tensor_copy(
                zbv[:, :, 1 + TBS * tb:1 + TBS * (tb + 1)],
                zv[:, :, 1 + TBS * tb:1 + TBS * (tb + 1)]),
                reads=[zR[k2] for k2 in range(8)],
                writes=[zbTR[tb]] + wouR + [xoRp[1]], tag=f"zmir{tb}"))
        plan.flush()
        dump_fm("h2", lambda k: (zv[:, k:k + 1, 1:T + 1], [zR[k]]))

        if dev_phase < 6:
            early_exit()
            return nc

        DEN2 = f32(OFF_AUX + 8192, 2048)
        rtab_t = rtab.ap()

        def z_col(k, tb, shifted):
            return 2049 * k + 512 * tb + (0 if shifted and k >= 4 else 1)

        def zt_f32(k, tb, shifted=False):
            return f32(OFF_Z + 4 * z_col(k, tb, shifted), 512)

        def zbt(k, tb, shifted=False):
            return b16(OFF_ACT + 2 * z_col(k, tb, shifted), 512)

        for lf in range(DEPTH_F):
            L = f"L{lf}_"
            fqkv_k = fqkv.ap()[lf].rearrange("(k p) f -> k p f", p=128)
            fwo_k = fwo.ap()[lf].rearrange("(k p) f -> k p f", p=128)
            fw1_k = fw1.ap()[lf].rearrange("(k p) f -> k p f", p=128)
            fw2_full = fw2.ap()[lf, 0:2688].rearrange("(k p) f -> k p f", p=128)
            fw2_tail = fw2.ap()[lf, 2688:2730]

            qaC = [[C.Ra(L + f"qa{pl}_{cb2}",
                          OFF_QA + 4096 * pl + 1024 * cb2, 1024)
                     for cb2 in range(4)] for pl in range(4)]
            kaC = [[C.Ra(L + f"ka{pl}_{cb2}",
                          OFF_KA + 4096 * pl + 1024 * cb2, 1024)
                     for cb2 in range(4)] for pl in range(4)]
            zmirTR = C.zbTR

            TBL = b16(OFF_BIG, 8192)
            tblR = C.Ra(L + "tbl", OFF_BIG, 16384)
            HI = f32(OFF_BIG + 16384, 2048)
            LO = f32(OFF_BIG + 24576, 2048)
            hiR = C.Ra(L + "hi", OFF_BIG + 16384, 8192)
            loR = C.Ra(L + "lo", OFF_BIG + 24576, 8192)
            for t4, nm in enumerate(("qs", "qss", "ks", "kss")):
                for half in range(2):
                    C.dma("sync", HI[64 * half:64 * half + 64, :],
                          rtab_t[:, 2 * t4, :], writes=[hiR], tag=L + "rth")
                    C.dma("sync", LO[64 * half:64 * half + 64, :],
                          rtab_t[:, 2 * t4 + 1, :], writes=[loR], tag=L + "rtl")
                C.vop(lambda e: e.tensor_add(HI, HI, LO),
                      reads=[loR], writes=[hiR], tag=L + f"rtadd{t4}")
                col = smcol(f"{nm}{lf}")
                C.vop(lambda e, t4=t4, col=col: e.tensor_scalar(
                    TBL[:, 2048 * t4:2048 * (t4 + 1)], HI, col, None, ALU.mult),
                    reads=[R("sm"), hiR], writes=[tblR], tag=L + f"rtsc{t4}")

            wqkvR = [C.Ra(L + f"wqs{j}", OFF_BIG + 16384 + 2048 * j, 2048)
                     for j in range(8)]
            qkv_slots = {}
            for j in range(8):
                wsl = b16(OFF_BIG + 16384 + 2048 * j, 1024)
                C.dma("gpsimd", wsl.rearrange("p (k c) -> p k c", k=8),
                      fqkv_k[:, :, 128 * j:128 * (j + 1)].transpose([1, 0, 2]),
                      writes=[wqkvR[j]], tag=L + f"wqkv{j}")
                qkv_slots[j] = (wsl, wqkvR[j])

            iters = [(tb, qk, pl) for tb in range(NTB)
                     for qk in range(2) for pl in range(4)]

            def qkv_mm(i):
                tb, qk, pl = iters[i]
                j = 4 * qk + pl
                wsl, wjR = qkv_slots[j]
                pb = i % 3
                rdm = [zmirTR[tb]] + ([zmirTR[tb - 1]] if tb > 0 else [])
                for kt in range(8):
                    C.mm(bank(pb), wsl[:, 128 * kt:128 * (kt + 1)],
                         zbt(kt, tb, shifted=True),
                         start=(kt == 0), stop=(kt == 7),
                         reads=[wjR] + rdm,
                         writes=[psR[pb]], inc=(kt == 7),
                         tag=L + f"qkv{j}_{tb}")

            def chain_slots(i):
                p3 = i % 3
                return dict(
                    RAW=f32(OFF_AUX + 4096 * p3, 512),
                    SH=f32(OFF_AUX + 4096 * p3 + 2048, 512),
                    QSQ=b16(OFF_SC + 2048 * p3, 512),
                    RQ=f32(OFF_SC + 2048 * p3, 512),
                    rawR=C.Ra(L + f"raw{p3}", OFF_AUX + 4096 * p3, 2048),
                    shR=C.Ra(L + f"sh{p3}", OFF_AUX + 4096 * p3 + 2048, 2048),
                    qsqR=C.Ra(L + f"qsq{p3}", OFF_SC + 2048 * p3, 2048))

            def qkv_chainA(i):
                tb, qk, pl = iters[i]
                j = 4 * qk + pl
                pb = i % 3
                s = chain_slots(i)
                RAW, SH, QSQ = s["RAW"], s["SH"], s["QSQ"]
                rawR, shR, qsqR = s["rawR"], s["shR"], s["qsqR"]
                C.act(QSQ, bank(pb), AF.Square,
                      reads=[psR[pb]], writes=[qsqR], tag=L + f"sq{j}{tb}")
                C.act(RAW, bank(pb), AF.Identity,
                      reads=[psR[pb]], writes=[rawR], tag=L + f"raw{j}{tb}")
                pq = 4 + (i % 3)
                C.mm(bank(pq), bones[:, :], QSQ, start=True, stop=True,
                     reads=[qsqR, R("bones")], writes=[psR[pq]],
                     inc=True, tag=L + f"ss{j}{tb}")
                for (d0, s0) in ((0, 32), (32, 0), (64, 96), (96, 64)):
                    C.dma("sync", SH[d0:d0 + 32, :], RAW[s0:s0 + 32, :],
                          reads=[rawR], writes=[shR], tag=L + f"shf{j}{tb}")

            def qkv_chainB(i):
                tb, qk, pl = iters[i]
                j = 4 * qk + pl
                s = chain_slots(i)
                RAW, SH, RQ = s["RAW"], s["SH"], s["RQ"]
                rawR, shR, qsqR = s["rawR"], s["shR"], s["qsqR"]
                pq = 4 + (i % 3)
                C.vop(lambda e, pq=pq, RQ=RQ: e.tensor_scalar(
                    RQ, bank(pq), 1e-24, None, ALU.max),
                    reads=[psR[pq]], writes=[qsqR], tag=L + f"ssm{j}{tb}")
                C.act(RQ, RQ, AF.Ln, reads=[qsqR], writes=[qsqR],
                      tag=L + f"ssln{j}{tb}")
                C.act(RQ, RQ, AF.Exp, scale=fc(-0.5),
                      reads=[qsqR, R("fconst")], writes=[qsqR],
                      tag=L + f"ssex{j}{tb}")
                tcq = TBL[:, 4096 * qk + 512 * tb:4096 * qk + 512 * tb + 512]
                tsq = TBL[:, 4096 * qk + 2048 + 512 * tb:
                          4096 * qk + 2048 + 512 * tb + 512]
                dst = (qa_pair(pl) if qk == 0 else ka_pair(pl))
                dstR = (qaC if qk == 0 else kaC)[pl][tb]
                p4 = i % 4
                M1 = f32(OFF_PT + 2048 * p4, 512)
                m1R = C.Ra(L + f"m1_{p4}", OFF_PT + 2048 * p4, 2048)
                C.vop(lambda e, tcq=tcq, RAW=RAW, M1=M1: e.tensor_mul(
                    M1, RAW, tcq),
                    reads=[rawR, tblR], writes=[m1R], tag=L + f"ro1{j}{tb}")
                C.vop(lambda e, tsq=tsq, SH=SH: e.scalar_tensor_tensor(
                    SH, SH, 1.0, tsq, ALU.mult, ALU.mult),
                    reads=[shR, tblR], writes=[shR], tag=L + f"ro2{j}{tb}")
                C.vop(lambda e, M1=M1, SH=SH: e.tensor_add(M1, M1, SH),
                      reads=[shR], writes=[m1R], tag=L + f"ro3{j}{tb}")
                C.vop(lambda e, dst=dst, tb=tb, M1=M1, RQ=RQ: e.tensor_mul(
                    dst[:, 512 * tb:512 * (tb + 1)], M1, RQ),
                    reads=[m1R, qsqR], writes=[dstR],
                    tag=L + f"ro4{j}{tb}")

            for i in range(2):
                qkv_mm(i)
                qkv_chainA(i)
            for i in range(len(iters)):
                if i + 2 < len(iters):
                    qkv_mm(i + 2)
                    qkv_chainA(i + 2)
                qkv_chainB(i)
            vfR = C.Ra(L + "vf", OFF_BIG, 24576)
            vones2 = b16(OFF_BIG, 12288).rearrange(
                "p (a c) -> p a c", c=192)[:, :, 64:128]
            C.vop(lambda e, v=vones2: e.memset(v, 1.0), writes=[vfR],
                  tag=L + "vones2")

            def vf_lhsT(kt, pl, odd):
                return b16(OFF_BIG + 2 * (kt * 768 + pl * 192 + (64 if odd else 0)), 128)

            for jv in range(4):
                slot = next_slot()
                wsl = stream_w_tile(fqkv_k, 1024 + 128 * jv, slot, L + f"wv{jv}")
                for tt in range(16):
                    pb = 3 if tt % 2 == 0 else 7
                    for kt in range(8):
                        C.mm(bank(pb)[:, 0:128],
                             b16(OFF_ACT + 2 * (2049 * kt + 128 * tt +
                                                (0 if kt >= 4 else 1)), 128),
                             wsl[:, 128 * kt:128 * (kt + 1)],
                             start=(kt == 0), stop=(kt == 7),
                             reads=[wrRa(slot), zmirTR[tt // 4]] +
                                   ([zmirTR[tt // 4 - 1]] if tt >= 4 else []),
                             writes=[psR[pb]], inc=(kt == 7),
                             tag=L + f"v{jv}{tt}")
                    for odd in range(2):
                        dstv = b16(OFF_BIG + 2 * (tt * 768 + jv * 192 + 128 * odd), 64)
                        C.vop(lambda e, dstv=dstv, pb=pb, odd=odd: e.tensor_copy(
                            dstv, bank(pb)[:, 64 * odd:64 * odd + 64]),
                            reads=[psR[pb]], writes=[vfR], tag=L + f"vev{jv}{tt}")

            woWR = [C.Ra(L + f"woW{j}", OFF_ACT + 2048 * j, 2048)
                    for j in range(8)]
            for j in range(8):
                wsl = b16(OFF_ACT + 2048 * j, 512)
                C.dma("gpsimd", wsl.rearrange("p (k c) -> p k c", k=4),
                      fwo_k[:, :, 128 * j:128 * (j + 1)].transpose([1, 0, 2]),
                      writes=[woWR[j]] + zmirTR, tag=L + f"wow{j}")

            denR4 = [C.Ra(L + f"den{pl}", OFF_AUX + 2048 * pl, 2048)
                     for pl in range(4)]
            dtmR4 = [C.Ra(L + f"dtm{pl}", OFF_AUX + 8192 + 1024 * pl, 1024)
                     for pl in range(4)]
            fillers = []

            def pop_filler():
                _, _, fn = fillers.pop(0)
                fn()

            def drain_den(pl):
                while any(k == "den" and p == pl for k, p, _ in fillers):
                    pop_filler()

            def wo_quantum(j, tb):
                pb = j % 2
                wsl = b16(OFF_ACT + 2048 * j, 512)
                for kt in range(4):
                    C.mm(bank(pb), wsl[:, 128 * kt:128 * (kt + 1)],
                         qa_pair(kt)[:, 512 * tb:512 * (tb + 1)],
                         start=(kt == 0), stop=(kt == 3),
                         reads=[woWR[j], qaC[kt][tb]], writes=[psR[pb]],
                         inc=(kt == 3), tag=L + f"wo{j}{tb}")
                zt = zt_f32(j, tb)
                C.vop(lambda e, zt=zt, pb=pb: e.tensor_add(zt, bank(pb), zt),
                      reads=[psR[pb], zR[j]], writes=[zR[j]],
                      tag=L + f"wores{j}{tb}")

            for qb in range(NTB):
                ktlo = max(0, 4 * qb - 4)
                kthi = 4 * qb + 3
                kts = [4 * qb] + [t for t in range(ktlo, kthi + 1)
                                  if t != 4 * qb]
                ns = len(kts)
                for pl in range(4):
                    for step in range(ns + 1):
                        if step < ns:
                            kt = kts[step]
                            b0 = 2 if step % 2 == 0 else 4
                            for odd in range(2):
                                r0 = 64 * odd
                                C.mm(bank(b0 + odd),
                                     ka_pair(pl)[r0:r0 + 64,
                                                 128 * kt:128 * (kt + 1)],
                                     qa_pair(pl)[r0:r0 + 64,
                                                 512 * qb:512 * (qb + 1)],
                                     start=True, stop=True,
                                     reads=[qaC[pl][qb], kaC[pl][kt // 4]],
                                     writes=[psR[b0 + odd]],
                                     inc=(odd == 1), tile_position=(r0, 0),
                                     tag=L + f"st{pl}_{qb}_{kt}")
                        if fillers and step % 2 == 1:
                            pop_filler()
                        if step > 0:
                            kt = kts[step - 1]
                            d0 = 128 * kt - 512 * qb
                            lo = max(0, d0)
                            hi = min(512, d0 + 640)
                            b0 = 2 if (step - 1) % 2 == 0 else 4
                            p4a = (step - 1) % 4
                            ptb = OFF_PT + 2048 * p4a
                            ptuR = C.Ra(f"ptc{p4a}", ptb, 2048)
                            ptap = b16(ptb, 1024)
                            src3 = ps[:, 512 * b0:512 * b0 + 1024].rearrange(
                                "p (a c) -> p a c", c=512)[:, :, lo:hi]
                            dst3 = b16(ptb, 1024).rearrange(
                                "p (a c) -> p a c", c=512)[:, :, lo:hi]
                            C.act(dst3, src3, AF.Exp,
                                  reads=[psR[b0], psR[b0 + 1]],
                                  writes=[ptuR], tag=L + f"exp{pl}{qb}{kt}")
                            mi = 0 if d0 >= 0 else 1
                            f0 = d0 if d0 >= 0 else d0 + 512
                            for odd in range(2):
                                sl = ptap[:, 512 * odd + f0:512 * odd + f0 + 128]
                                C.vop(lambda e, sl=sl, mi=mi: e.tensor_mul(
                                    sl, sl, mk[:, 128 * mi:128 * mi + 128]),
                                    reads=[R("mk")], writes=[ptuR],
                                    tag=L + f"msk{pl}{qb}{kt}")
                            for odd in range(2):
                                C.mm(ps[:, 512 * (6 + odd) + lo:
                                        512 * (6 + odd) + hi],
                                     vf_lhsT(kt, pl, odd),
                                     b16(ptb + 2 * (512 * odd + lo), hi - lo),
                                     start=(kt == kts[0]), stop=(kt == kts[-1]),
                                     reads=[ptuR, vfR],
                                     writes=[psR[6 + odd]],
                                     inc=(kt == kts[-1]),
                                     tag=L + f"pv{pl}_{qb}_{kt}")
                    drain_den(pl)
                    DEN = f32(OFF_AUX + 2048 * pl, 512)
                    TMP = f32(OFF_AUX + 8192 + 1024 * pl, 512)
                    ot = qa_pair(pl)
                    denR = denR4[pl]
                    dtmR = dtmR4[pl]
                    C.vop(lambda e, ot=ot, qb=qb: e.tensor_copy(
                        ot[0:64, 512 * qb:512 * (qb + 1)], bank(6)[0:64, :]),
                        reads=[psR[6]], writes=[qaC[pl][qb]],
                        tag=L + f"oev{pl}{qb}")
                    C.vop(lambda e, ot=ot, qb=qb: e.tensor_copy(
                        ot[64:128, 512 * qb:512 * (qb + 1)],
                        bank(7)[64:128, :]),
                        reads=[psR[7]], writes=[qaC[pl][qb]],
                        tag=L + f"oev2{pl}{qb}")
                    C.vop(lambda e, DEN=DEN: e.tensor_copy(
                        DEN[64:128, :], bank(6)[64:128, :]),
                        reads=[psR[6]], writes=[denR], tag=L + f"dev{pl}{qb}")
                    C.vop(lambda e, DEN=DEN: e.tensor_copy(
                        DEN[0:64, :], bank(7)[0:64, :]),
                        reads=[psR[7]], writes=[denR], tag=L + f"dev2{pl}{qb}")
                    C.dma("sync", TMP[0:64, :], DEN[0:64, :],
                          reads=[denR], writes=[dtmR], tag=L + f"dsw{pl}{qb}")
                    C.dma("sync", DEN[0:64, :], DEN[64:128, :],
                          reads=[denR, dtmR], writes=[denR],
                          tag=L + f"dsw2{pl}{qb}")
                    C.dma("sync", DEN[64:128, :], TMP[0:64, :],
                          reads=[dtmR], writes=[denR], tag=L + f"dsw3{pl}{qb}")

                    def den_norm(pl=pl, qb=qb, DEN=DEN, denR=denR, ot=ot):
                        C.act(DEN, DEN, AF.Ln, reads=[denR], writes=[denR],
                              tag=L + f"dln{pl}{qb}")
                        C.act(DEN, DEN, AF.Exp, scale=fc(-1.0),
                              reads=[denR, R("fconst")], writes=[denR],
                              tag=L + f"dex{pl}{qb}")
                        C.vop(lambda e, ot=ot, qb=qb, DEN=DEN: e.tensor_mul(
                            ot[:, 512 * qb:512 * (qb + 1)],
                            ot[:, 512 * qb:512 * (qb + 1)], DEN),
                            reads=[denR], writes=[qaC[pl][qb]],
                            tag=L + f"on{pl}{qb}")

                    fillers.append(("den", pl, den_norm))
                for j in range(8):
                    fillers.append(
                        ("wo", None, lambda j=j, tb=qb: wo_quantum(j, tb)))
            while fillers:
                pop_filler()
            plan.flush()
            dump_fm(f"fo{lf}", lambda pl: (qa_pair(pl), qaC[pl]), nk=4)
            plan.flush()
            dump_fm(f"z{lf}a", lambda k: (zv[:, k:k + 1, 1:T + 1], [zR[k]]))
            hlnR = [[C.Ra(L + f"hln{k}_{tb}",
                          ((OFF_QA + 4096 * k) if k < 4 else
                           (OFF_KA + 4096 * (k - 4))) + 1024 * tb, 1024)
                     for tb in range(NTB)] for k in range(8)]
            lnxbR = [C.Ra(L + "lnxb0", OFF_BIG, 8192),
                     C.Ra(L + "lnxb1", OFF_BIG + 8192, 8192)]
            lnxqR = C.Ra(L + "lnxq", OFF_BIG + 16384, 8192)
            XBF = [b16(OFF_BIG, 4096), b16(OFF_BIG + 8192, 4096)]
            XQF = b16(OFF_BIG + 16384, 4096)

            def hln_ap(k, tb):
                base = (OFF_QA + 4096 * k) if k < 4 else (OFF_KA + 4096 * (k - 4))
                return b16(base + 1024 * tb, 512)

            def ffln_prefetch(tb):
                for grp in range(2):
                    ks, ke = 4 * grp, 4 * grp + 4
                    C.vop(lambda e, tb=tb, grp=grp, ks=ks, ke=ke: e.tensor_copy(
                        XBF[tb % 2][:, 2048 * grp:2048 * (grp + 1)]
                        .rearrange("p (k t) -> p k t", k=4),
                        zv[:, ks:ke,
                           512 * tb + (1 if grp == 0 else 0):
                           512 * tb + (1 if grp == 0 else 0) + 512]),
                        reads=[zR[k2] for k2 in range(ks, ke)],
                        writes=[lnxbR[tb % 2]], tag=L + f"lnld{tb}")

            layernorm(
                f32_src=lambda k, tb: (zt_f32(k, tb, shifted=True), [zR[k]]),
                bf_src=lambda tb: (XBF[tb % 2], [lnxbR[tb % 2]]),
                xq_ap=XQF, xqR=lnxqR,
                scale_col=lambda k: smcol(f"ln_g{lf}", k),
                bias_col=lambda k: smcol(f"ln_b{lf}", k),
                out_dst=lambda k, tb: (hln_ap(k, tb), [hlnR[k][tb]], []),
                tag=L + "ln", prefetch=ffln_prefetch)
            plan.flush()
            dump_fm(f"hln{lf}", lambda k: (
                b16(((OFF_QA + 4096 * k) if k < 4 else
                     (OFF_KA + 4096 * (k - 4))), 2048), hlnR[k]))

            gatedR = C.Ra(L + "gated", OFF_BIG, 32768)
            GELB = b16(OFF_SC, 512)
            mlist = list(range(22))
            chunks = [mlist[0:8], mlist[8:16], mlist[16:22]]
            for ci, msub in enumerate(chunks):
                for mi, m in enumerate(msub):
                    wd = 128 if m < 21 else 42
                    ca0 = 256 * m if m < 21 else 5376
                    sa = next_slot()
                    wa = stream_w_tile(fw1_k, ca0, sa, L + f"w1a{m}", width=wd)
                    sg = next_slot()
                    wg = stream_w_tile(fw1_k, ca0 + wd, sg, L + f"w1g{m}",
                                       width=wd)
                    for tb in range(NTB):
                        pw = ((4 * mi + tb) % 2) * 2
                        for kt in range(8):
                            C.mm(bank(pw)[0:wd, :], wa[:, wd * kt:wd * (kt + 1)],
                                 hln_ap(kt, tb), start=(kt == 0), stop=(kt == 7),
                                 reads=[wrRa(sa), hlnR[kt][tb]], writes=[psR[pw]],
                                 inc=(kt == 7), tag=L + f"w1a{m}{tb}")
                            C.mm(bank(pw + 1)[0:wd, :], wg[:, wd * kt:wd * (kt + 1)],
                                 hln_ap(kt, tb), start=(kt == 0), stop=(kt == 7),
                                 reads=[wrRa(sg), hlnR[kt][tb]], writes=[psR[pw + 1]],
                                 inc=(kt == 7), tag=L + f"w1g{m}{tb}")
                        par = (4 * mi + tb) % 2
                        GELc = b16(OFF_SC + 1024 * par, 512)
                        gelR = C.Ra(L + f"gel{par}", OFF_SC + 1024 * par, 1024)
                        C.act(GELc[0:wd, :], bank(pw + 1)[0:wd, :], AF.Gelu,
                              bias=smcol(f"b1{lf}", 2 * m + 1)[0:wd, :],
                              reads=[psR[pw + 1], R("sm")], writes=[gelR],
                              tag=L + f"gel{m}{tb}")
                        gt = b16(OFF_BIG + 2 * (2048 * mi + 512 * tb), 512)
                        C.vop(lambda e, gt=gt, wd=wd, m=m, pw=pw, GELc=GELc:
                              e.scalar_tensor_tensor(
                                  gt[0:wd, :], bank(pw)[0:wd, :],
                                  smcol(f"b1{lf}", 2 * m)[0:wd, :], GELc[0:wd, :],
                                  ALU.add, ALU.mult),
                              reads=[psR[pw], R("sm"), gelR],
                              writes=[gatedR], tag=L + f"gate{m}{tb}")
                for j in range(8):
                    slot = next_slot()
                    wsl = wr_slot(slot)
                    nfull = sum(1 for m in msub if m < 21)
                    if nfull:
                        C.dma("gpsimd",
                              wsl[:, 0:nfull * 128].rearrange(
                                  "p (k c) -> p k c", k=nfull),
                              fw2_full[msub[0]:msub[0] + nfull, :,
                                       128 * j:128 * (j + 1)].transpose([1, 0, 2]),
                              writes=[wrRa(slot)], tag=L + f"w2{ci}{j}")
                    if len(msub) > nfull:
                        C.dma("gpsimd",
                              wsl[0:42, nfull * 128:nfull * 128 + 128],
                              fw2_tail[:, 128 * j:128 * (j + 1)],
                              writes=[wrRa(slot)], tag=L + f"w2t{j}")
                    for tb in range(NTB):
                        pb = 4 + ((2 * j + tb) % 4)
                        for mi, m in enumerate(msub):
                            wd = 128 if m < 21 else 42
                            gt = b16(OFF_BIG + 2 * (2048 * mi + 512 * tb), 512)
                            C.mm(bank(pb), wsl[0:wd, 128 * mi:128 * (mi + 1)],
                                 gt[0:wd, :],
                                 start=(mi == 0), stop=(mi == len(msub) - 1),
                                 reads=[wrRa(slot), gatedR], writes=[psR[pb]],
                                 inc=(mi == len(msub) - 1),
                                 tag=L + f"w2{ci}{j}{tb}")
                        zt = zt_f32(j, tb)
                        if ci < 2:
                            C.vop(lambda e, zt=zt, pb=pb: e.tensor_add(
                                zt, bank(pb), zt),
                                reads=[psR[pb], zR[j]], writes=[zR[j]],
                                tag=L + f"w2res{ci}{j}{tb}")
                        else:
                            C.vop(lambda e, zt=zt, pb=pb, j=j:
                                  e.scalar_tensor_tensor(
                                      zt, bank(pb), smcol(f"b2{lf}", j), zt,
                                      ALU.add, ALU.add),
                                  reads=[psR[pb], R("sm"), zR[j]],
                                  writes=[zR[j]], tag=L + f"w2res{ci}{j}{tb}")
            plan.flush()
            dump_fm(f"z{lf}", lambda k: (zv[:, k:k + 1, 1:T + 1], [zR[k]]))
            if lf + 1 < DEPTH_F:
                for tb in range(NTB):
                    C.vop(lambda e, tb=tb: e.tensor_copy(
                        zbv[:, :, 1 + TBS * tb:1 + TBS * (tb + 1)],
                        zv[:, :, 1 + TBS * tb:1 + TBS * (tb + 1)]),
                        reads=[zR[k2] for k2 in range(8)],
                        writes=[C.zbTR[tb]], tag=L + f"zmir{tb}")
            plan.flush()


        if dev_phase < 7:
            early_exit()
            return nc

        XFp = [f32(OFF_AUX, 4096), f32(OFF_QA, 4096)]
        OUTSp = [f32(OFF_BIG, 4096), f32(OFF_BIG + 16384, 4096)]
        xfRf = [C.Ra("fin_xf0", OFF_AUX, 16384),
                C.Ra("fin_xf1", OFF_QA, 16384)]
        outR = [C.Ra("fin_out0", OFF_BIG, 16384),
                C.Ra("fin_out1", OFF_BIG + 16384, 16384)]
        youtR = C.R("yout")
        for tb in range(NTB):
            XF = XFp[tb % 2]
            OUTS = OUTSp[tb % 2]
            C.dma("sync", XF.rearrange("p (k t) -> p k t", k=8),
                  xs_k[:, :, TBS * tb:TBS * (tb + 1)].transpose([1, 0, 2]),
                  reads=[xscR], writes=[xfRf[tb % 2]], tag=f"fldx{tb}")
            for j in range(8):
                C.vop(lambda e, j=j, tb=tb, XF=XF, OUTS=OUTS:
                      e.scalar_tensor_tensor(
                          OUTS[:, 512 * j:512 * (j + 1)], zt_f32(j, tb),
                          modv[:, 72 + j:73 + j], XF[:, 512 * j:512 * (j + 1)],
                          ALU.mult, ALU.add),
                      reads=[zR[j], R("modd"), xfRf[tb % 2]],
                      writes=[outR[tb % 2]],
                      tag=f"fin{j}{tb}")
            C.dma("gpsimd",
                  y_k[:, :, TBS * tb:TBS * (tb + 1)].transpose([1, 0, 2]),
                  OUTS.rearrange("p (k t) -> p k t", k=8),
                  reads=[outR[tb % 2]], writes=[youtR], tag=f"fst{tb}")
        plan.flush()

        def final_wait(e):
            for name in list(plan.sems):
                if plan.cnt.get(name, 0) > 0:
                    e.wait_ge(plan.sems[name], plan.cnt[name])

        plan.pending["sync"].append(final_wait)
        plan.flush()

    return nc



def make_in_maps(inputs, host):
    x = np.asarray(inputs["L0_attn_x"], np.float32)
    c = np.asarray(inputs["L0_c"], np.float32)
    maps = []
    for b in range(NCORES):
        maps.append(dict(
            xt=np.ascontiguousarray(x[b].T),
            wqt=host["wqt"], wot=host["wot"], fqkv=host["fqkv"],
            fwo=host["fwo"], fw1=host["fw1"], fw2=host["fw2"],
            wmod=host["wmod"],
            cvec=np.ascontiguousarray(c[b].reshape(8, 128).T),
            small=host["small"], rtab=host["rtab"], masks=host["masks"],
        ))
    return maps


def kernel(**inputs):
    host = _host_prep(inputs)
    nc = build_program()
    in_maps = make_in_maps(inputs, host)
    res = run_bass_kernel_spmd(nc, in_maps, core_ids=list(range(NCORES)))
    out = np.stack([np.asarray(r["y"], np.float32).T for r in res.results])
    return np.ascontiguousarray(out)


# version 28
# speedup vs baseline: 1.0192x; 1.0192x over previous
import math

import numpy as np
import ml_dtypes

import concourse.bass as bass
import concourse.mybir as mybir
from concourse.bass_utils import run_bass_kernel_spmd

AF = mybir.ActivationFunctionType
ALU = mybir.AluOpType
F32 = mybir.dt.float32
BF16 = mybir.dt.bfloat16
BFNP = ml_dtypes.bfloat16

NCORES = 8
B, T, D = 8, 2048, 1024
NTB = 4
TBS = 512
HD = 64
NHA = 16
NHF = 8
WIN = 512
FINNER = 512
HID = 2730
FD1 = 2 * HID
EPS = 1e-5
DEPTH_F = 2

STRICT = False

OFF_Z = 0
OFF_ACT = 65568
OFF_QA = 98352
OFF_KA = 114736
OFF_BIG = 131120
OFF_PT = 163888
OFF_WR = 167984
OFF_SC = 172080
OFF_AUX = 178224
ARENA_BYTES = 206896



def _rotary_tables_f64():
    half = np.arange(0, HD, 2, dtype=np.float64)
    inv_freq = 1.0 / 10000.0 ** (half / HD)
    t = np.arange(T, dtype=np.float64)
    freqs = t[:, None] * inv_freq[None, :]
    freqs = np.concatenate([freqs, freqs], axis=-1)
    base = (half + 0.4 * HD) / (1.4 * HD)
    power = (t - T // 2) / float(WIN // 2)
    scale = base[None, :] ** power[:, None]
    scale = np.concatenate([scale, scale], axis=-1)
    cos, sin = np.cos(freqs), np.sin(freqs)
    sgn = np.where(np.arange(HD) < 32, -1.0, 1.0)[None, :]
    return ((cos * scale).T, (sgn * sin * scale).T,
            (cos / scale).T, (sgn * sin / scale).T)


def _hi_lo(a64):
    hi = a64.astype(np.float32)
    return hi, (a64 - hi.astype(np.float64)).astype(np.float32)


def _host_masks():
    p = np.arange(128)[:, None]
    f = np.arange(128)[None, :]
    out = np.zeros((128, 2, 128), dtype=np.float32)
    out[:, 0, :] = (f >= p).astype(np.float32)
    out[:, 1, :] = (f <= p).astype(np.float32)
    return out


def _w1_perm():
    perm, widths = [], []
    for m in range(21):
        perm.extend(range(128 * m, 128 * m + 128)); widths.append(128)
        perm.extend(range(HID + 128 * m, HID + 128 * m + 128)); widths.append(128)
    perm.extend(range(2688, HID)); widths.append(42)
    perm.extend(range(HID + 2688, FD1)); widths.append(42)
    return np.asarray(perm), widths


_W1PERM, _W1WIDTHS = _w1_perm()

SMALL_COLS = {}


def _build_small(inp):
    cols = []

    def add(name, arr2d):
        SMALL_COLS[name] = (len(cols), arr2d.shape[1])
        for i in range(arr2d.shape[1]):
            cols.append(arr2d[:, i])

    def tile_vec(v, width_list=None):
        if width_list is None:
            nt = (len(v) + 127) // 128
            width_list = [min(128, len(v) - 128 * j) for j in range(nt)]
        out = np.zeros((128, len(width_list)), dtype=np.float32)
        o = 0
        for j, w in enumerate(width_list):
            out[:w, j] = v[o:o + w]
            o += w
        return out

    for l in range(DEPTH_F):
        add(f"ln_g{l}", tile_vec(np.asarray(inp["L0_ffn_ln_g"][l], np.float32)))
        add(f"ln_b{l}", tile_vec(np.asarray(inp["L0_ffn_ln_b"][l], np.float32)))
        b1p = np.asarray(inp["L0_ffn_b1"][l], np.float32)[_W1PERM]
        add(f"b1{l}", tile_vec(b1p, _W1WIDTHS))
        add(f"b2{l}", tile_vec(np.asarray(inp["L0_ffn_b2"][l], np.float32)))
        qs = np.asarray(inp["L0_ffn_q_scale"][l], np.float32)
        ks = np.asarray(inp["L0_ffn_k_scale"][l], np.float32)
        add(f"qs{l}", np.concatenate([qs, qs])[:, None])
        add(f"qss{l}", np.concatenate([qs[32:], qs[:32], qs[32:], qs[:32]])[:, None])
        add(f"ks{l}", np.concatenate([ks, ks])[:, None])
        add(f"kss{l}", np.concatenate([ks[32:], ks[:32], ks[32:], ks[:32]])[:, None])
    return np.ascontiguousarray(np.stack(cols, axis=1).astype(np.float32))


def _host_prep(inp):
    h = {}
    h["wqt"] = np.ascontiguousarray(np.asarray(inp["L0_attn_w_qkv"], np.float32).T)
    h["wot"] = np.ascontiguousarray(np.asarray(inp["L0_attn_w_out"], np.float32).T)
    h["fqkv"] = np.ascontiguousarray(np.asarray(inp["L0_ffn_wqkv"], np.float32))
    h["fwo"] = np.ascontiguousarray(np.asarray(inp["L0_ffn_wo"], np.float32))
    h["fw1"] = np.ascontiguousarray(np.asarray(inp["L0_ffn_w1"], np.float32)[:, :, _W1PERM])
    h["fw2"] = np.ascontiguousarray(np.asarray(inp["L0_ffn_w2"], np.float32))
    h["wmod"] = np.ascontiguousarray(np.asarray(inp["L0_wmod"], np.float32))
    h["small"] = _build_small(inp)
    tabs = []
    for tbl in _rotary_tables_f64():
        hi, lo = _hi_lo(tbl)
        tabs.extend((hi, lo))
    h["rtab"] = np.ascontiguousarray(np.stack(tabs, axis=1).astype(np.float32))
    h["masks"] = _host_masks()
    return h



ENGS = ("tensor", "scalar", "vector", "gpsimd", "sync")


class Region:
    __slots__ = ("name", "lo", "hi", "w", "readers", "dead", "initdeps")

    def __init__(self, name, lo=-1, hi=-1):
        self.name = name
        self.lo, self.hi = lo, hi
        self.w = None
        self.readers = []
        self.dead = False
        self.initdeps = []

    def __repr__(self):
        return f"R({self.name})"


class Plan:
    def __init__(self, nc, block, strict=STRICT):
        self.nc = nc
        self.block = block
        self.strict = strict
        self.sems = {}
        self.cnt = {e: 0 for e in ENGS}
        self.seq = {e: 0 for e in ENGS}
        self.pending = {e: [] for e in ENGS}
        self.watermark = {e: {} for e in ENGS}
        self.incpts = {e: [] for e in ENGS}
        self.allwaits = []
        self.oplog = {e: [] for e in ENGS}
        self.nops = 0

    def _sem(self, name):
        if name not in self.sems:
            self.sems[name] = self.nc.semaphore(f"s_{name}").__enter__()
        return self.sems[name]

    def op(self, eng, fn, reads=(), writes=(), inc=False, dma=False, tag=""):
        deps = {}
        dseq = {}
        loose = []

        def add_dep(tok):
            if tok is None:
                return
            key, v, exact, was_dma, sq = tok
            if key == eng and key == "tensor" and not was_dma and not self.strict:
                return
            if was_dma and key == "gpsimd" and eng != "gpsimd":
                v += 16
            if not exact:
                loose.append((key, sq, tok))
                return
            if v > deps.get(key, 0):
                deps[key] = v
            dseq[key] = max(dseq.get(key, -1), sq)

        for r in reads:
            assert not r.dead, f"read of dead region {r.name} in {tag}"
            add_dep(r.w)
            for t in r.initdeps:
                add_dep(t)
        for r in writes:
            assert not r.dead, f"write of dead region {r.name} in {tag}"
            add_dep(r.w)
            for t in r.readers:
                add_dep(t)
            for t in r.initdeps:
                add_dep(t)

        for key, sq, t in loose:
            if dseq.get(key, -1) >= sq:
                continue
            import bisect
            pts = self.incpts[key]
            i = bisect.bisect_left(pts, (sq, -1))
            if i >= len(pts):
                raise RuntimeError(
                    f"dep on non-inc'ed {key} op (seq {sq}) needed by "
                    f"{eng}:{tag} with no covering inc — add inc=True")
            v = pts[i][1]
            if v > deps.get(key, 0):
                deps[key] = v
            dseq[key] = max(dseq.get(key, -1), pts[i][0])

        if deps.get("gpsimd", 0) > self.cnt["gpsimd"] and not (
                eng == "gpsimd" and dma):
            self.op("gpsimd", self.swdge_dummy, dma=True, tag="swdge_cover")
        if dma or self.strict:
            inc = True
        if inc:
            self.cnt[eng] += 16 if dma else 1
        self.seq[eng] += 1
        if inc:
            self.incpts[eng].append((self.seq[eng], self.cnt[eng]))
        tok = (eng, self.cnt[eng], inc, dma, self.seq[eng])
        waits = []
        for key, v in deps.items():
            if v > self.watermark[eng].get(key, 0):
                waits.append((self._sem(key), v))
                self.watermark[eng][key] = v
                self.allwaits.append((eng, key, v, tag))
        my_inc = (self._sem(eng), 16 if dma else 1) if inc else None
        self.nops += 1

        def emit(e, waits=waits, fn=fn, my_inc=my_inc):
            for s, v in waits:
                e.wait_ge(s, v)
            inst = fn(e)
            if my_inc is not None and inst is not None:
                inst.then_inc(my_inc[0], my_inc[1])

        self.pending[eng].append(emit)
        self.oplog[eng].append((dict(deps), (16 if dma else 1) if inc else 0, tag))
        for r in reads:
            r.readers.append(tok)
        for r in writes:
            r.w = tok
            r.readers = []
            r.initdeps = []
        return tok

    def flush(self):
        for e in ENGS:
            ops = self.pending[e]
            if not ops:
                continue
            self.pending[e] = []

            def body(engine, ops=ops):
                for f in ops:
                    f(engine)

            getattr(self.block, e)(body)


class Ctx:
    def __init__(self, nc, plan):
        self.nc = nc
        self.p = plan
        self.reg = {}
        self.ranged = []

    def R(self, name, takes=()):
        if name in self.reg:
            return self.reg[name]
        r = Region(name)
        for o in takes:
            o = self.reg[o] if isinstance(o, str) else o
            o.dead = True
            if o.w is not None:
                r.initdeps.append(o.w)
            r.initdeps.extend(o.readers)
        self.reg[name] = r
        return r

    def Ra(self, name, lo, size):
        r = self.reg.get(name)
        if r is not None and not r.dead:
            assert r.lo == lo and r.hi == lo + size, (name, lo, size, r.lo, r.hi)
            return r
        r = Region(name, lo, lo + size)
        for o in self.ranged:
            if o.hi > lo and o.lo < lo + size:
                if o.w is not None:
                    r.initdeps.append(o.w)
                r.initdeps.extend(o.readers)
                o.dead = True
        self.ranged.append(r)
        self.reg[name] = r
        return r

    def dma(self, queue, out_ap, in_ap, reads=(), writes=(), tag=""):
        return self.p.op(queue, lambda e: e.dma_start(out_ap, in_ap),
                         reads=reads, writes=writes, dma=True, tag=tag)

    def mm(self, out_ap, lhsT, rhs, start, stop, reads=(), writes=(),
           inc=False, tile_position=None, tag=""):
        return self.p.op(
            "tensor",
            lambda e: e.matmul(out_ap, lhsT, rhs, start=start, stop=stop,
                               tile_position=tile_position),
            reads=reads, writes=writes, inc=inc, tag=tag)

    def act(self, out_ap, in_ap, func, bias=0.0, scale=1.0, accum_out=None,
            reads=(), writes=(), inc=True, tag=""):
        return self.p.op(
            "scalar",
            lambda e: e.activation(out_ap, in_ap, func, bias=bias, scale=scale,
                                   accum_out=accum_out),
            reads=reads, writes=writes, inc=inc, tag=tag)

    def vop(self, fn, reads=(), writes=(), inc=True, eng="vector", tag=""):
        return self.p.op(eng, fn, reads=reads, writes=writes, inc=inc, tag=tag)



def build_program(dev_phase=99, dev_debug=None, strict=STRICT):
    dev_debug = dev_debug or {}
    nc = bass.Bass("TRN2", target_bir_lowering=False,
                   detect_race_conditions=False)

    xt = nc.dram_tensor("xt", [D, T], F32, kind="ExternalInput")
    wqt = nc.dram_tensor("wqt", [D, 3 * D], F32, kind="ExternalInput")
    wot = nc.dram_tensor("wot", [D, D], F32, kind="ExternalInput")
    fqkv = nc.dram_tensor("fqkv", [2, D, 3 * FINNER], F32, kind="ExternalInput")
    fwo = nc.dram_tensor("fwo", [2, FINNER, D], F32, kind="ExternalInput")
    fw1 = nc.dram_tensor("fw1", [2, D, FD1], F32, kind="ExternalInput")
    fw2 = nc.dram_tensor("fw2", [2, HID, D], F32, kind="ExternalInput")
    wmod = nc.dram_tensor("wmod", [D, 6 * D], F32, kind="ExternalInput")
    cvec = nc.dram_tensor("cvec", [128, 8], F32, kind="ExternalInput")
    NS = max(1, sum(v[1] for v in SMALL_COLS.values()))
    small = nc.dram_tensor("small", [128, NS], F32, kind="ExternalInput")
    rtab = nc.dram_tensor("rtab", [64, 8, T], F32, kind="ExternalInput")
    masks = nc.dram_tensor("masks", [128, 2, 128], F32, kind="ExternalInput")
    y = nc.dram_tensor("y", [D, T], F32, kind="ExternalOutput")
    xsc = nc.dram_tensor("xsc", [D, T], F32,
                         kind="ExternalOutput" if dev_debug else "Internal")
    dbgs = {}
    for name in dev_debug:
        dbgs[name] = nc.dram_tensor(f"dbg_{name}", [D, T], F32,
                                    kind="ExternalOutput")

    xt_k = xt.ap().rearrange("(k p) t -> k p t", p=128)
    xs_k = xsc.ap().rearrange("(k p) t -> k p t", p=128)
    y_k = y.ap().rearrange("(k p) t -> k p t", p=128)

    import contextlib
    with contextlib.ExitStack() as stack:
        arn = stack.enter_context(
            nc.sbuf_tensor("arn", [128, ARENA_BYTES // 2], BF16))
        sm = stack.enter_context(nc.sbuf_tensor("sm", [128, NS], F32))
        cvs = stack.enter_context(nc.sbuf_tensor("cvs", [128, 32], F32))
        cvb = stack.enter_context(nc.sbuf_tensor("cvb", [128, 8], BF16))
        modv = stack.enter_context(nc.sbuf_tensor("modv", [128, 80], F32))
        fci = stack.enter_context(nc.sbuf_tensor("fci", [128, 24], F32))
        ones = stack.enter_context(nc.sbuf_tensor("ones", [128, 128], BF16))
        bones = stack.enter_context(nc.sbuf_tensor("bones", [128, 128], BF16))
        mk = stack.enter_context(nc.sbuf_tensor("mk", [128, 256], BF16))
        ps = stack.enter_context(nc.psum_tensor("ps", [128, 4096], F32))
        block = stack.enter_context(nc.Block())

        plan = Plan(nc, block, strict=strict)
        plan.swdge_dummy = lambda e: e.dma_start(cvs[0:1, 30:32],
                                                 masks.ap()[0:1, 0, 0:2])
        C = Ctx(nc, plan)
        R = C.R

        def b16(off, words):
            assert off % 2 == 0
            return arn[:, off // 2: off // 2 + words]

        def f32(off, words):
            assert off % 4 == 0
            return arn[:, off // 2: off // 2 + 2 * words].bitcast(F32)

        def bank(i):
            return ps[:, 512 * i:512 * (i + 1)]

        psR = [R(f"ps{i}") for i in range(8)]

        fc_map = {}

        def fc(val):
            if val not in fc_map:
                i = len(fc_map)
                assert i < 24
                C.vop(lambda e, i=i, val=val: e.memset(fci[:, i:i + 1], val),
                      writes=[R("fconst")], tag=f"fc{val}")
                fc_map[val] = fci[:, i:i + 1]
            return fc_map[val]

        def smcol(name, j=0):
            c0, w = SMALL_COLS[name]
            assert j < w, (name, j, w)
            return sm[:, c0 + j:c0 + j + 1]

        def dump_fm(name, src_cb, nk=8):
            if name not in dbgs:
                return
            dk = dbgs[name].ap().rearrange("(k p) t -> k p t", p=128)
            for k in range(nk):
                ap, reads = src_cb(k)
                C.dma("gpsimd", dk[k], ap, reads=reads,
                      writes=[C.R(f"dbgo{name}")], tag=f"dbgd{name}")
            plan.flush()

        C.vop(lambda e: e.memset(ones[:, :], 1.0), writes=[R("ones")])
        C.vop(lambda e: e.memset(bones[:, :], 0.0), writes=[R("bones")])
        C.vop(lambda e: e.memset(bones[0:64, 0:64], 1.0), writes=[R("bones")])
        C.vop(lambda e: e.memset(bones[64:128, 64:128], 1.0), writes=[R("bones")])
        C.dma("sync", sm[:, :], small.ap(), writes=[R("sm")])
        C.dma("sync", cvs[:, 0:8], cvec.ap(), writes=[R("cv")])
        C.dma("gpsimd", mk[:, :].rearrange("p (a b) -> p a b", a=2), masks.ap(),
              writes=[R("mk")])
        C.act(cvs[:, 8:16], cvs[:, 0:8], AF.Silu, reads=[R("cv")], writes=[R("cvs")])
        C.vop(lambda e: e.tensor_copy(cvb[:, :], cvs[:, 8:16]),
              reads=[R("cvs")], writes=[R("cvb")])

        xfA = f32(OFF_AUX, 4096)
        xfB = f32(OFF_Z + 49152, 4096)
        xfR = [C.Ra("xf0", OFF_AUX, 16384), C.Ra("xf1", OFF_Z + 49152, 16384)]
        xbR = [C.Ra("xb0", OFF_Z + 24576, 8192), C.Ra("xb1", OFF_Z + 32768, 8192)]
        xqR = C.Ra("ln1xq", OFF_Z + 40960, 8192)
        h1R = [[C.Ra(f"h1_{k}_{tb}", OFF_ACT + 4096 * k + 1024 * tb, 1024)
                for tb in range(NTB)] for k in range(8)]
        ACTB = b16(OFF_ACT, 16392)

        def ln1_prefetch(tb):
            xfap = xfA if tb % 2 == 0 else xfB
            srcap = xt_k[:, :, TBS * tb:TBS * (tb + 1)].transpose([1, 0, 2])
            C.dma("sync", xfap.rearrange("p (k t) -> p k t", k=8), srcap,
                  writes=[xfR[tb % 2]], tag=f"ldx{tb}")
            C.vop(lambda e, tb=tb, xfap=xfap: e.tensor_copy(
                b16(OFF_Z + 24576 + 8192 * (tb % 2), 4096), xfap),
                reads=[xfR[tb % 2]], writes=[xbR[tb % 2]], tag=f"ldxb{tb}")

        ln1_prefetch(0)
        ln1_prefetch(1)

        wmod_k = wmod.ap().rearrange("(k p) f -> k p f", p=128)
        def wrRa(slot):
            if slot < 2:
                return C.Ra(f"wr{slot}", OFF_WR + 2048 * slot, 2048)
            if slot < 4:
                return C.Ra(f"wr{slot}", OFF_PT + 2048 * (slot - 2), 2048)
            if slot < 7:
                return C.Ra(f"wr{slot}", OFF_SC + 2048 * (slot - 4), 2048)
            return C.Ra(f"wr{slot}", OFF_AUX + 24576 + 2048 * (slot - 7), 2048)

        def wr_slot(slot):
            if slot < 2:
                return b16(OFF_WR + 2048 * slot, 1024)
            if slot < 4:
                return b16(OFF_PT + 2048 * (slot - 2), 1024)
            if slot < 7:
                return b16(OFF_SC + 2048 * (slot - 4), 1024)
            return b16(OFF_AUX + 24576 + 2048 * (slot - 7), 1024)

        for j in range(48):
            slot = j % 2
            wsl = wr_slot(slot)
            C.dma("gpsimd", wsl.rearrange("p (k c) -> p k c", k=8),
                  wmod_k[:, :, 128 * j:128 * (j + 1)].transpose([1, 0, 2]),
                  writes=[wrRa(slot)], tag=f"wmod{j}")
            for k in range(8):
                C.mm(ps[:, 3072 + j:3073 + j], wsl[:, 128 * k:128 * (k + 1)],
                     cvb[:, k:k + 1], start=(k == 0), stop=(k == 7),
                     reads=[wrRa(slot), R("cvb")], writes=[psR[6]],
                     inc=(k == 7), tag=f"mod{j}")
        C.vop(lambda e: e.tensor_copy(modv[:, 0:48], ps[:, 3072:3120]),
              reads=[psR[6]], writes=[R("modv")], tag="modevac")
        C.vop(lambda e: e.tensor_scalar(modv[:, 48:56], modv[:, 8:16], 1.0, None, ALU.add),
              reads=[R("modv")], writes=[R("modd")])
        C.vop(lambda e: e.tensor_scalar(modv[:, 56:64], modv[:, 32:40], 1.0, None, ALU.add),
              reads=[R("modv")], writes=[R("modd")])
        for (src, dst) in ((16, 64), (40, 72)):
            C.vop(lambda e, s=src: e.tensor_scalar(
                cvs[:, 16:24], modv[:, s:s + 8], 0.0, 2.0, ALU.is_ge, ALU.mult),
                reads=[R("modv")], writes=[R("sgn")])
            C.vop(lambda e: e.tensor_scalar(
                cvs[:, 16:24], cvs[:, 16:24], -1.0, None, ALU.add),
                reads=[R("sgn")], writes=[R("sgn")])
            C.vop(lambda e, s=src: e.tensor_scalar(
                cvs[:, 24:32], modv[:, s:s + 8], -1.0, None, ALU.mult),
                reads=[R("modv")], writes=[R("mag")])
            C.vop(lambda e, s=src: e.tensor_max(
                cvs[:, 24:32], cvs[:, 24:32], modv[:, s:s + 8]),
                reads=[R("modv"), R("mag")], writes=[R("mag")])
            C.vop(lambda e: e.tensor_scalar(
                cvs[:, 24:32], cvs[:, 24:32], 0.25, None, ALU.max),
                reads=[R("mag")], writes=[R("mag")])
            C.vop(lambda e, d=dst: e.tensor_mul(
                modv[:, d:d + 8], cvs[:, 16:24], cvs[:, 24:32]),
                reads=[R("sgn"), R("mag")], writes=[R("modd")])
        plan.flush()

        def layernorm(f32_src, bf_src, xq_ap, xqR, scale_col, bias_col,
                      out_dst, tag, prefetch=None, post_tb=None):
            if prefetch is not None:
                prefetch(0)
                if NTB > 1:
                    prefetch(1)
            for tb in range(NTB):
                par = tb % 2
                pb = par * 2
                mu = f32(OFF_AUX + 16384 + 4096 * par, 512)
                varb = f32(OFF_AUX + 16384 + 4096 * par + 2048, 512)
                muR = C.Ra(tag + f"mu{par}", OFF_AUX + 16384 + 4096 * par, 2048)
                vaR = C.Ra(tag + f"va{par}", OFF_AUX + 18432 + 4096 * par, 2048)
                bfap, bfreads = bf_src(tb)
                C.act(xq_ap, bfap, AF.Square, reads=bfreads,
                      writes=[xqR], tag=f"{tag}sq{tb}")
                for k in range(8):
                    C.mm(bank(pb), ones[:, :], bfap[:, 512 * k:512 * (k + 1)],
                         start=(k == 0), stop=(k == 7),
                         reads=list(bfreads) + [R("ones")], writes=[psR[pb]],
                         inc=(k == 7), tag=f"{tag}sum{tb}")
                    C.mm(bank(pb + 1), ones[:, :], xq_ap[:, 512 * k:512 * (k + 1)],
                         start=(k == 0), stop=(k == 7),
                         reads=[xqR, R("ones")], writes=[psR[pb + 1]],
                         inc=(k == 7), tag=f"{tag}ssq{tb}")
                C.vop(lambda e, mu=mu, pb=pb: e.tensor_scalar(
                    mu, bank(pb), 1.0 / D, None, ALU.mult),
                    reads=[psR[pb]], writes=[muR], tag=f"{tag}mu{tb}")
                C.vop(lambda e, mu=mu, varb=varb: e.tensor_mul(varb, mu, mu),
                      reads=[muR], writes=[vaR], tag=f"{tag}mu2{tb}")
                C.vop(lambda e, varb=varb, pb=pb: e.scalar_tensor_tensor(
                    varb, bank(pb + 1), 1.0 / D, varb, ALU.mult, ALU.subtract),
                    reads=[psR[pb + 1], vaR], writes=[vaR], tag=f"{tag}var{tb}")
                C.act(varb, varb, AF.Ln, bias=fc(EPS),
                      reads=[vaR, R("fconst")], writes=[vaR],
                      tag=f"{tag}ln{tb}")
                C.act(varb, varb, AF.Exp, scale=fc(-0.5),
                      reads=[vaR, R("fconst")], writes=[vaR],
                      tag=f"{tag}exp{tb}")
                C.vop(lambda e, mu=mu, varb=varb: e.tensor_mul(mu, mu, varb),
                      reads=[vaR], writes=[muR], tag=f"{tag}b3{tb}")
                for k in range(8):
                    sl = 2 * par + (k % 2)
                    off1 = (OFF_PT + 2048 * sl) if sl < 2 else \
                        (OFF_AUX + 24576 + 2048 * (sl - 2))
                    m1 = f32(off1, 512)
                    m1R = C.Ra(tag + f"m1_{sl}", off1, 2048)
                    fap, freads = f32_src(k, tb)
                    C.vop(lambda e, m1=m1, fap=fap, varb=varb: e.tensor_mul(
                        m1, fap, varb),
                        reads=list(freads) + [vaR], writes=[m1R],
                        tag=f"{tag}a1_{tb}_{k}")
                    C.vop(lambda e, m1=m1, mu=mu: e.tensor_sub(m1, m1, mu),
                          reads=[muR], writes=[m1R], tag=f"{tag}a2_{tb}_{k}")
                    oap, owrites, oreads = out_dst(k, tb)
                    C.act(oap, m1, AF.Identity, scale=scale_col(k), bias=bias_col(k),
                          reads=[m1R, R("modd"), R("sm")] + list(oreads),
                          writes=owrites, tag=f"{tag}a3_{tb}_{k}")
                if post_tb is not None:
                    post_tb(tb)
                if prefetch is not None and tb + 2 < NTB:
                    prefetch(tb + 2)

        layernorm(
            f32_src=lambda k, tb: ((xfA if tb % 2 == 0 else xfB)
                                   [:, 512 * k:512 * (k + 1)], [xfR[tb % 2]]),
            bf_src=lambda tb: (b16(OFF_Z + 24576 + 8192 * (tb % 2), 4096),
                               [xbR[tb % 2]]),
            xq_ap=b16(OFF_Z + 40960, 4096), xqR=xqR,
            scale_col=lambda k: modv[:, 48 + k:49 + k],
            bias_col=lambda k: modv[:, k:k + 1],
            out_dst=lambda k, tb: (ACTB[:, 2048 * k + 512 * tb:
                                        2048 * k + 512 * tb + 512],
                                   [h1R[k][tb]], []),
            tag="ln1", prefetch=lambda tb: (ln1_prefetch(tb)
                                            if tb >= 2 else None))
        plan.flush()

        dump_fm("h1", lambda k: (ACTB[:, 2048 * k:2048 * (k + 1)], h1R[k]))

        if dev_phase < 3:
            early_exit()
            return nc

        OFF_DV = OFF_Z
        OFF_DO = OFF_Z + 24576
        dvR = C.Ra("dv", OFF_Z, 24576)
        doR = [C.Ra(f"do{gp}", OFF_Z + 24576 + 4096 * gp, 4096)
               for gp in range(8)]
        DEN = f32(OFF_AUX, 2048)
        DENS = f32(OFF_AUX + 8192, 2048)
        wqt_k = wqt.ap().rearrange("(k p) f -> k p f", p=128)

        def ov_pair(gp):
            return b16(OFF_DO + 4096 * gp, 2048)

        def qa_pair(pl):
            return b16(OFF_QA + 4096 * pl, 2048)

        def ka_pair(pl):
            return b16(OFF_KA + 4096 * pl, 2048)

        def v_lhsT(kt, pl, odd):
            return b16(OFF_DV + 2 * (kt * 768 + pl * 192 + (64 if odd else 0)), 128)

        qaR = [C.Ra(f"qa{pl}", OFF_QA + 4096 * pl, 4096) for pl in range(4)]
        kaR = [C.Ra(f"ka{pl}", OFF_KA + 4096 * pl, 4096) for pl in range(4)]

        def ptRa(i):
            return C.Ra(f"pt{i}", OFF_PT + 2048 * i, 2048)

        vones = b16(OFF_DV, 12288).rearrange("p (a c) -> p a c", c=192)[:, :, 64:128]
        C.vop(lambda e: e.memset(vones, 1.0), writes=[dvR], tag="vones")

        def stream_w_tile(src_k, col0, slot, tagp, nk=8, width=128):
            wsl = wr_slot(slot)
            C.dma("gpsimd",
                  wsl[:, 0:nk * width].rearrange("p (k c) -> p k c", k=nk),
                  src_k[:, :, col0:col0 + width].transpose([1, 0, 2]),
                  writes=[wrRa(slot)], tag=tagp)
            return wsl

        sl_ctr = [1]

        def next_slot():
            sl_ctr[0] ^= 1
            return sl_ctr[0]

        def emit_qk_tile(w, qk, pl):
            jf = 1024 * qk + 128 * (4 * w + pl)
            slot = next_slot()
            wsl = stream_w_tile(wqt_k, jf, slot, f"wq{w}{qk}{pl}")
            dstR = (qaR if qk == 0 else kaR)[pl]
            dst = (qa_pair if qk == 0 else ka_pair)(pl)
            for tb in range(NTB):
                pb = tb % 2
                for kt in range(8):
                    C.mm(bank(pb), wsl[:, 128 * kt:128 * (kt + 1)],
                         ACTB[:, 2048 * kt + 512 * tb:2048 * kt + 512 * tb + 512],
                         start=(kt == 0), stop=(kt == 7),
                         reads=[wrRa(slot), h1R[kt]],
                         writes=[psR[pb]], inc=(kt == 7),
                         tag=f"qk{w}{qk}{pl}{tb}")
                if qk == 0:
                    C.vop(lambda e, dst=dst, tb=tb, pb=pb: e.tensor_scalar(
                        dst[:, 512 * tb:512 * (tb + 1)], bank(pb),
                        0.125, None, ALU.mult),
                        reads=[psR[pb]], writes=[dstR],
                        tag=f"qkev{w}{qk}{pl}{tb}")
                else:
                    C.vop(lambda e, dst=dst, tb=tb, pb=pb: e.tensor_copy(
                        dst[:, 512 * tb:512 * (tb + 1)], bank(pb)),
                        reads=[psR[pb]], writes=[dstR],
                        tag=f"qkev{w}{qk}{pl}{tb}")

        def emit_v_tile(w, jv, vbase, vR):
            jf = 2048 + 512 * w + 128 * jv
            slot = next_slot()
            wsl = stream_w_tile(wqt_k, jf, slot, f"wv{w}{jv}")
            for tt in range(16):
                pb = tt % 2
                for kt in range(8):
                    C.mm(bank(pb)[:, 0:128],
                         ACTB[:, 2048 * kt + 128 * tt:2048 * kt + 128 * tt + 128],
                         wsl[:, 128 * kt:128 * (kt + 1)],
                         start=(kt == 0), stop=(kt == 7),
                         reads=[wrRa(slot), h1R[kt]],
                         writes=[psR[pb]], inc=(kt == 7),
                         tag=f"v{w}{jv}{tt}")
                dstv2 = b16(vbase + 2 * (tt * 768 + jv * 192), 256).rearrange(
                    "p (a c) -> p a c", c=128)[:, :, 0:64]
                C.vop(lambda e, dstv2=dstv2, pb=pb: e.tensor_copy(
                    dstv2, bank(pb)[:, 0:128].rearrange(
                        "p (a c) -> p a c", c=64)),
                    reads=[psR[pb]], writes=[vR], tag=f"vev{w}{jv}{tt}")


        def dense_attention_pair(chains, vbase, vR):
            tmpR = C.Ra("dentmp", OFF_AUX + 16384, 8192)
            TMP = f32(OFF_AUX + 16384, 2048)
            denRs = {}
            for ci, (pl, gp, bm) in enumerate(chains):
                denRs[ci] = C.Ra(f"den{ci}", bm["den"], 8192)

            def s_banks(bm, kt):
                return bm["s"][kt % 2]

            def pt_ap(bm, kt, words=1024):
                return b16(bm["ptb"] + 2048 * (kt % 2), words)

            def ptR(bm, kt):
                return C.Ra(f"pt{bm['ptb']}_{kt % 2}", bm["ptb"] + 2048 * (kt % 2), 2048)

            for qb in range(NTB):
                nkt = 4 * qb + 4
                for step in range(nkt + 1):
                    for ci, (pl, gp, bm) in enumerate(chains):
                        if step > 0:
                            kt = step - 1
                            d0 = max(0, 128 * kt - 512 * qb)
                            p = kt % 2
                            ptb = OFF_PT + 4096 * p + 2048 * ci
                            ptuR = C.Ra(f"ptu{p}_{ci}", ptb, 2048)
                            ptap = b16(ptb, 1024)
                            b0, b1 = s_banks(bm, kt)
                            src3 = ps[:, 512 * b0:512 * b0 + 1024].rearrange(
                                "p (a c) -> p a c", c=512)[:, :, d0:512]
                            dst3 = b16(ptb, 1024).rearrange(
                                "p (a c) -> p a c", c=512)[:, :, d0:512]
                            C.act(dst3, src3, AF.Exp,
                                  reads=[psR[b0], psR[b1]],
                                  writes=[ptuR], tag=f"exp{gp}_{qb}_{kt}")
                            if 128 * kt - 512 * qb >= 0:
                                for odd in range(2):
                                    sl = ptap[:, 512 * odd + d0:512 * odd + d0 + 128]
                                    C.vop(lambda e, sl=sl: e.tensor_mul(
                                        sl, sl, mk[:, 0:128]),
                                        reads=[R("mk")], writes=[ptuR],
                                        tag=f"msk{gp}_{qb}_{kt}")
                            for odd in range(2):
                                C.mm(ps[:, 512 * (bm["pv"] + odd) + d0:
                                        512 * (bm["pv"] + odd) + 512],
                                     b16(vbase + 2 * (kt * 768 + pl * 192 +
                                                      (64 if odd else 0)), 128),
                                     b16(ptb + 2 * (512 * odd + d0), 512 - d0),
                                     start=(kt == 0), stop=(kt == nkt - 1),
                                     reads=[ptuR, vR],
                                     writes=[psR[bm["pv"] + odd]],
                                     inc=(kt == nkt - 1),
                                     tag=f"pv{gp}_{qb}_{kt}")
                        if step < nkt:
                            kt = step
                            b0, b1 = s_banks(bm, kt)
                            for odd in range(2):
                                r0 = 64 * odd
                                C.mm(bank(b1 if odd else b0),
                                     ka_pair(pl)[r0:r0 + 64, 128 * kt:128 * (kt + 1)],
                                     qa_pair(pl)[r0:r0 + 64, 512 * qb:512 * (qb + 1)],
                                     start=True, stop=True,
                                     reads=[qaR[pl], kaR[pl]],
                                     writes=[psR[b1 if odd else b0]],
                                     inc=(odd == 1), tile_position=(r0, 0),
                                     tag=f"st{gp}_{qb}_{kt}")
                    if step == nkt:
                        for ci, (pl, gp, bm) in enumerate(chains):
                            denR = denRs[ci]
                            DENc = f32(bm["den"], 2048)
                            ot = ov_pair(gp)
                            pv = bm["pv"]
                            C.vop(lambda e, ot=ot, qb=qb, pv=pv: e.tensor_copy(
                                ot[0:64, 512 * qb:512 * (qb + 1)], bank(pv)[0:64, :]),
                                reads=[psR[pv]], writes=[doR[gp]], tag=f"oev{gp}_{qb}")
                            C.vop(lambda e, ot=ot, qb=qb, pv=pv: e.tensor_copy(
                                ot[64:128, 512 * qb:512 * (qb + 1)],
                                bank(pv + 1)[64:128, :]),
                                reads=[psR[pv + 1]], writes=[doR[gp]],
                                tag=f"oev2{gp}_{qb}")
                            C.vop(lambda e, qb=qb, pv=pv, DENc=DENc: e.tensor_copy(
                                DENc[64:128, 512 * qb:512 * (qb + 1)],
                                bank(pv)[64:128, :]),
                                reads=[psR[pv]], writes=[denR], tag=f"dev{gp}_{qb}")
                            C.vop(lambda e, qb=qb, pv=pv, DENc=DENc: e.tensor_copy(
                                DENc[0:64, 512 * qb:512 * (qb + 1)],
                                bank(pv + 1)[0:64, :]),
                                reads=[psR[pv + 1]], writes=[denR],
                                tag=f"dev2{gp}_{qb}")
            for ci, (pl, gp, bm) in enumerate(chains):
                denR = denRs[ci]
                DENc = f32(bm["den"], 2048)
                C.dma("sync", TMP[0:64, :], DENc[0:64, :],
                      reads=[denR], writes=[tmpR], tag=f"dsw{gp}")
                C.dma("sync", DENc[0:64, :], DENc[64:128, :],
                      reads=[denR, tmpR], writes=[denR], tag=f"dsw2{gp}")
                C.dma("sync", DENc[64:128, :], TMP[0:64, :],
                      reads=[tmpR], writes=[denR], tag=f"dsw3{gp}")
                C.act(DENc, DENc, AF.Ln, reads=[denR], writes=[denR],
                      tag=f"dln{gp}")
                C.act(DENc, DENc, AF.Exp, scale=fc(-1.0),
                      reads=[denR, R("fconst")], writes=[denR], tag=f"dex{gp}")
                C.vop(lambda e, gp=gp, DENc=DENc: e.tensor_mul(
                    ov_pair(gp), ov_pair(gp), DENc),
                    reads=[denR], writes=[doR[gp]], tag=f"onorm{gp}")

        BM_A = dict(s=((2, 3), (2, 3)), pv=6, ptb=OFF_PT, den=OFF_AUX)
        BM_B = dict(s=((4, 5), (4, 5)), pv=0, ptb=OFF_WR, den=OFF_AUX + 8192)
        w0R = [C.Ra(f"dw0_{j}", OFF_BIG + 2048 * j, 2048) for j in range(12)]
        for sj, jf in enumerate([128 * j for j in range(4)] +
                                [1024 + 128 * j for j in range(4)] +
                                [2048 + 128 * j for j in range(4)]):
            wsl = b16(OFF_BIG + 2048 * sj, 1024)
            C.dma("gpsimd", wsl.rearrange("p (k c) -> p k c", k=8),
                  wqt_k[:, :, jf:jf + 128].transpose([1, 0, 2]),
                  writes=[w0R[sj]], tag=f"dw0_{sj}")

        for tb in range(NTB):
            for qk in range(2):
                for pl in range(4):
                    sj = 4 * qk + pl
                    wsl = b16(OFF_BIG + 2048 * sj, 1024)
                    dstR = (qaR if qk == 0 else kaR)[pl]
                    dst = (qa_pair if qk == 0 else ka_pair)(pl)
                    pb = sj % 2
                    for kt in range(8):
                        C.mm(bank(pb), wsl[:, 128 * kt:128 * (kt + 1)],
                             ACTB[:, 2048 * kt + 512 * tb:2048 * kt + 512 * tb + 512],
                             start=(kt == 0), stop=(kt == 7),
                             reads=[w0R[sj], h1R[kt][tb]],
                             writes=[psR[pb]], inc=(kt == 7),
                             tag=f"qk0{qk}{pl}{tb}")
                    if qk == 0:
                        C.vop(lambda e, dst=dst, tb=tb, pb=pb: e.tensor_scalar(
                            dst[:, 512 * tb:512 * (tb + 1)], bank(pb),
                            0.125, None, ALU.mult),
                            reads=[psR[pb]], writes=[dstR],
                            tag=f"qkev0{qk}{pl}{tb}")
                    else:
                        C.vop(lambda e, dst=dst, tb=tb, pb=pb: e.tensor_copy(
                            dst[:, 512 * tb:512 * (tb + 1)], bank(pb)),
                            reads=[psR[pb]], writes=[dstR],
                            tag=f"qkev0{qk}{pl}{tb}")
        for jv in range(4):
            wsl = b16(OFF_BIG + 2048 * (8 + jv), 1024)
            for tt in range(16):
                pb = tt % 2
                for kt in range(8):
                    C.mm(bank(pb)[:, 0:128],
                         ACTB[:, 2048 * kt + 128 * tt:2048 * kt + 128 * tt + 128],
                         wsl[:, 128 * kt:128 * (kt + 1)],
                         start=(kt == 0), stop=(kt == 7),
                         reads=[w0R[8 + jv], h1R[kt][tt // 4]],
                         writes=[psR[pb]], inc=(kt == 7),
                         tag=f"v0{jv}{tt}")
                dstv2 = b16(OFF_DV + 2 * (tt * 768 + jv * 192), 256).rearrange(
                    "p (a c) -> p a c", c=128)[:, :, 0:64]
                C.vop(lambda e, dstv2=dstv2, pb=pb: e.tensor_copy(
                    dstv2, bank(pb)[:, 0:128].rearrange(
                        "p (a c) -> p a c", c=64)),
                    reads=[psR[pb]], writes=[dvR], tag=f"vev0{jv}{tt}")

        def w1_qk_mm(pl, slot_q, slot_k):
            for qk, slot in ((0, slot_q), (1, slot_k)):
                wsl = wr_slot(slot)
                dstR = (qaR if qk == 0 else kaR)[pl]
                dst = (qa_pair if qk == 0 else ka_pair)(pl)
                for tb in range(NTB):
                    pb = tb % 2
                    for kt in range(8):
                        C.mm(bank(pb), wsl[:, 128 * kt:128 * (kt + 1)],
                             ACTB[:, 2048 * kt + 512 * tb:
                                  2048 * kt + 512 * tb + 512],
                             start=(kt == 0), stop=(kt == 7),
                             reads=[wrRa(slot), h1R[kt][tb]],
                             writes=[psR[pb]], inc=(kt == 7),
                             tag=f"qk1{qk}{pl}{tb}")
                    if qk == 0:
                        C.vop(lambda e, dst=dst, tb=tb, pb=pb: e.tensor_scalar(
                            dst[:, 512 * tb:512 * (tb + 1)], bank(pb),
                            0.125, None, ALU.mult),
                            reads=[psR[pb]], writes=[dstR],
                            tag=f"qkev1{qk}{pl}{tb}")
                    else:
                        C.vop(lambda e, dst=dst, tb=tb, pb=pb: e.tensor_copy(
                            dst[:, 512 * tb:512 * (tb + 1)], bank(pb)),
                            reads=[psR[pb]], writes=[dstR],
                            tag=f"qkev1{qk}{pl}{tb}")

        def w1_v_mm(jv, vR):
            wsl = b16(OFF_BIG + 2048 * (12 + jv), 1024)
            for tt in range(16):
                pb = tt % 2
                for kt in range(8):
                    C.mm(bank(pb)[:, 0:128],
                         ACTB[:, 2048 * kt + 128 * tt:2048 * kt + 128 * tt + 128],
                         wsl[:, 128 * kt:128 * (kt + 1)],
                         start=(kt == 0), stop=(kt == 7),
                         reads=[C.Ra(f"dw1v{jv}", OFF_BIG + 2048 * (12 + jv), 2048),
                                h1R[kt][tt // 4]],
                         writes=[psR[pb]], inc=(kt == 7),
                         tag=f"v1{jv}{tt}")
                dstv2 = b16(OFF_BIG + 2 * (tt * 768 + jv * 192), 256).rearrange(
                    "p (a c) -> p a c", c=128)[:, :, 0:64]
                C.vop(lambda e, dstv2=dstv2, pb=pb: e.tensor_copy(
                    dstv2, bank(pb)[:, 0:128].rearrange(
                        "p (a c) -> p a c", c=64)),
                    reads=[psR[pb]], writes=[vR], tag=f"vev1{jv}{tt}")

        dv1R = None
        slot_rot = [0]
        for blk in range(2):
            for pl in (2 * blk, 2 * blk + 1):
                slot_q = 4 + (slot_rot[0] % 5); slot_rot[0] += 1
                slot_k = 4 + (slot_rot[0] % 5); slot_rot[0] += 1
                stream_w_tile(wqt_k, 128 * (4 + pl), slot_q, f"dw1q{pl}")
                stream_w_tile(wqt_k, 1024 + 128 * (4 + pl), slot_k, f"dw1k{pl}")
                C.dma("gpsimd",
                      b16(OFF_BIG + 2048 * (12 + pl), 1024)
                      .rearrange("p (k c) -> p k c", k=8),
                      wqt_k[:, :, 2048 + 512 + 128 * pl:
                            2048 + 512 + 128 * pl + 128].transpose([1, 0, 2]),
                      writes=[C.Ra(f"dw1v{pl}", OFF_BIG + 2048 * (12 + pl), 2048)],
                      tag=f"dw1v{pl}")
                if pl == 2 * blk:
                    sq0, sk0 = slot_q, slot_k
                else:
                    sq1, sk1 = slot_q, slot_k
            dense_attention_pair(
                [(2 * blk, 2 * blk, BM_A), (2 * blk + 1, 2 * blk + 1, BM_B)],
                OFF_DV, dvR)
            w1_qk_mm(2 * blk, sq0, sk0)
            w1_qk_mm(2 * blk + 1, sq1, sk1)
            if blk == 0:
                dv1R = C.Ra("dv1", OFF_BIG, 24576)
                vones1 = b16(OFF_BIG, 12288).rearrange(
                    "p (a c) -> p a c", c=192)[:, :, 64:128]
                C.vop(lambda e, v=vones1: e.memset(v, 1.0), writes=[dv1R],
                      tag="vones1")
            w1_v_mm(2 * blk, dv1R)
            w1_v_mm(2 * blk + 1, dv1R)
        for blk in range(2):
            dense_attention_pair(
                [(2 * blk, 4 + 2 * blk, BM_A), (2 * blk + 1, 5 + 2 * blk, BM_B)],
                OFF_BIG, dv1R)

        plan.flush()
        dump_fm("o", lambda k: (ov_pair(k), [doR[k]]))
        if dev_phase < 4:
            early_exit()
            return nc

        wot_k = wot.ap().rearrange("(k p) f -> k p f", p=128)
        XS = f32(OFF_AUX, 4096)
        xsR = C.Ra("xs", OFF_AUX, 16384)
        xscR = C.R("xsc")
        XOp = [f32(OFF_BIG + 16384, 4096), f32(OFF_ACT + 16384, 4096)]
        xoRp = [C.Ra("xo0", OFF_BIG + 16384, 16384),
                C.Ra("xo1", OFF_ACT + 16384, 16384)]
        wouR = [C.Ra(f"wou{j}", OFF_ACT + 2048 * j, 2048) for j in range(8)]
        for j in range(8):
            wsl = b16(OFF_ACT + 2048 * j, 1024)
            C.dma("gpsimd", wsl.rearrange("p (k c) -> p k c", k=8),
                  wot_k[:, :, 128 * j:128 * (j + 1)].transpose([1, 0, 2]),
                  writes=[wouR[j]], tag=f"wo{j}")
        def wout_tb(tb):
            XO = XOp[tb % 2]
            C.dma("gpsimd", XS.rearrange("p (k t) -> p k t", k=8),
                  xt_k[:, :, TBS * tb:TBS * (tb + 1)].transpose([1, 0, 2]),
                  writes=[xsR], tag=f"woldx{tb}")
            for j in range(8):
                wsl = b16(OFF_ACT + 2048 * j, 1024)
                pb = j % 2
                for kt in range(8):
                    C.mm(bank(pb), wsl[:, 128 * kt:128 * (kt + 1)],
                         ov_pair(kt)[:, 512 * tb:512 * (tb + 1)],
                         start=(kt == 0), stop=(kt == 7),
                         reads=[wouR[j], doR[kt]], writes=[psR[pb]],
                         inc=(kt == 7), tag=f"wo{j}{tb}")
                C.vop(lambda e, j=j, pb=pb, XO=XO: e.scalar_tensor_tensor(
                    XO[:, 512 * j:512 * (j + 1)], bank(pb), modv[:, 64 + j:65 + j],
                    XS[:, 512 * j:512 * (j + 1)], ALU.mult, ALU.add),
                    reads=[psR[pb], R("modd"), xsR], writes=[xoRp[tb % 2]],
                    tag=f"wores{j}{tb}")
            C.dma("sync", xs_k[:, :, TBS * tb:TBS * (tb + 1)].transpose([1, 0, 2]),
                  XO.rearrange("p (k t) -> p k t", k=8),
                  reads=[xoRp[tb % 2]], writes=[xscR], tag=f"spill{tb}")

        for tb in range(NTB):
            wout_tb(tb)
        if dev_phase < 5:
            early_exit()
            return nc

        zR = [C.Ra(f"z{k}", OFF_Z + 4 * 2049 * k, 8196) for k in range(8)]
        zbTR = [C.R(f"zbt{tb}") for tb in range(NTB)]
        C.zbTR = zbTR
        ZB = b16(OFF_ACT, 16392)
        zv = f32(OFF_Z, 16392).rearrange("p (k t) -> p k t", t=T + 1)
        zbv = ZB.rearrange("p (k t) -> p k t", t=T + 1)
        zgR = C.R("zgv")
        C.vop(lambda e: e.memset(zv[:, :, 0:1], 0.0),
              reads=[], writes=[zgR] + zR, tag="zguard")
        C.vop(lambda e: e.memset(zbv[:, :, 0:1], 0.0),
              reads=[], writes=[zgR] + zbTR + wouR + [xoRp[1]],
              tag="zbguard")
        ln2xb = [C.Ra("ln2xb0", OFF_QA, 8192), C.Ra("ln2xb1", OFF_QA + 8192, 8192)]
        ln2xqR = C.Ra("ln2xq", OFF_KA, 8192)
        ln2fR = [C.Ra("ln2f0", OFF_AUX, 16384), C.Ra("ln2f1", OFF_BIG, 16384)]
        xfA2 = f32(OFF_AUX, 4096)
        xfB2 = f32(OFF_BIG, 4096)

        def z_tile(k, tb):
            return f32(OFF_Z + 4 * (2049 * k + 1 + 512 * tb), 512)

        def ln2_prefetch(tb):
            xfap = xfA2 if tb % 2 == 0 else xfB2
            srcap = xs_k[:, :, TBS * tb:TBS * (tb + 1)].transpose([1, 0, 2])
            C.dma("gpsimd" if tb % 2 == 0 else "sync",
                  xfap.rearrange("p (k t) -> p k t", k=8), srcap,
                  reads=[xscR], writes=[ln2fR[tb % 2]], tag=f"l2ldx{tb}")
            C.vop(lambda e, tb=tb, xfap=xfap: e.tensor_copy(
                b16(OFF_QA + 8192 * (tb % 2), 4096), xfap),
                reads=[ln2fR[tb % 2]], writes=[ln2xb[tb % 2]],
                tag=f"l2cast{tb}")

        layernorm(
            f32_src=lambda k, tb: ((xfA2 if tb % 2 == 0 else xfB2)
                                   [:, 512 * k:512 * (k + 1)],
                                   [ln2fR[tb % 2]]),
            bf_src=lambda tb: (b16(OFF_QA + 8192 * (tb % 2), 4096),
                               [ln2xb[tb % 2]]),
            xq_ap=b16(OFF_KA, 4096), xqR=ln2xqR,
            scale_col=lambda k: modv[:, 56 + k:57 + k],
            bias_col=lambda k: modv[:, 24 + k:25 + k],
            out_dst=lambda k, tb: (z_tile(k, tb), [zR[k]], [zgR]),
            tag="ln2", prefetch=ln2_prefetch,
            post_tb=lambda tb: C.vop(lambda e, tb=tb: e.tensor_copy(
                zbv[:, :, 1 + TBS * tb:1 + TBS * (tb + 1)],
                zv[:, :, 1 + TBS * tb:1 + TBS * (tb + 1)]),
                reads=[zR[k2] for k2 in range(8)],
                writes=[zbTR[tb]] + wouR + [xoRp[1]], tag=f"zmir{tb}"))
        plan.flush()
        dump_fm("h2", lambda k: (zv[:, k:k + 1, 1:T + 1], [zR[k]]))

        if dev_phase < 6:
            early_exit()
            return nc

        DEN2 = f32(OFF_AUX + 8192, 2048)
        rtab_t = rtab.ap()

        def z_col(k, tb, shifted):
            return 2049 * k + 512 * tb + (0 if shifted and k >= 4 else 1)

        def zt_f32(k, tb, shifted=False):
            return f32(OFF_Z + 4 * z_col(k, tb, shifted), 512)

        def zbt(k, tb, shifted=False):
            return b16(OFF_ACT + 2 * z_col(k, tb, shifted), 512)

        for lf in range(DEPTH_F):
            L = f"L{lf}_"
            fqkv_k = fqkv.ap()[lf].rearrange("(k p) f -> k p f", p=128)
            fwo_k = fwo.ap()[lf].rearrange("(k p) f -> k p f", p=128)
            fw1_k = fw1.ap()[lf].rearrange("(k p) f -> k p f", p=128)
            fw2_full = fw2.ap()[lf, 0:2688].rearrange("(k p) f -> k p f", p=128)
            fw2_tail = fw2.ap()[lf, 2688:2730]

            qaC = [[C.Ra(L + f"qa{pl}_{cb2}",
                          OFF_QA + 4096 * pl + 1024 * cb2, 1024)
                     for cb2 in range(4)] for pl in range(4)]
            kaC = [[C.Ra(L + f"ka{pl}_{cb2}",
                          OFF_KA + 4096 * pl + 1024 * cb2, 1024)
                     for cb2 in range(4)] for pl in range(4)]
            zmirTR = C.zbTR

            TBL = b16(OFF_BIG, 8192)
            tblR = C.Ra(L + "tbl", OFF_BIG, 16384)
            HI = f32(OFF_BIG + 16384, 2048)
            LO = f32(OFF_BIG + 24576, 2048)
            hiR = C.Ra(L + "hi", OFF_BIG + 16384, 8192)
            loR = C.Ra(L + "lo", OFF_BIG + 24576, 8192)
            for t4, nm in enumerate(("qs", "qss", "ks", "kss")):
                for half in range(2):
                    C.dma("sync", HI[64 * half:64 * half + 64, :],
                          rtab_t[:, 2 * t4, :], writes=[hiR], tag=L + "rth")
                    C.dma("sync", LO[64 * half:64 * half + 64, :],
                          rtab_t[:, 2 * t4 + 1, :], writes=[loR], tag=L + "rtl")
                C.vop(lambda e: e.tensor_add(HI, HI, LO),
                      reads=[loR], writes=[hiR], tag=L + f"rtadd{t4}")
                col = smcol(f"{nm}{lf}")
                C.vop(lambda e, t4=t4, col=col: e.tensor_scalar(
                    TBL[:, 2048 * t4:2048 * (t4 + 1)], HI, col, None, ALU.mult),
                    reads=[R("sm"), hiR], writes=[tblR], tag=L + f"rtsc{t4}")

            wqkvR = [C.Ra(L + f"wqs{j}", OFF_BIG + 16384 + 2048 * j, 2048)
                     for j in range(8)]
            qkv_slots = {}
            for j in range(8):
                wsl = b16(OFF_BIG + 16384 + 2048 * j, 1024)
                C.dma("gpsimd", wsl.rearrange("p (k c) -> p k c", k=8),
                      fqkv_k[:, :, 128 * j:128 * (j + 1)].transpose([1, 0, 2]),
                      writes=[wqkvR[j]], tag=L + f"wqkv{j}")
                qkv_slots[j] = (wsl, wqkvR[j])

            iters = [(tb, qk, pl) for tb in range(NTB)
                     for qk in range(2) for pl in range(4)]

            def qkv_mm(i):
                tb, qk, pl = iters[i]
                j = 4 * qk + pl
                wsl, wjR = qkv_slots[j]
                pb = i % 3
                rdm = [zmirTR[tb]] + ([zmirTR[tb - 1]] if tb > 0 else [])
                for kt in range(8):
                    C.mm(bank(pb), wsl[:, 128 * kt:128 * (kt + 1)],
                         zbt(kt, tb, shifted=True),
                         start=(kt == 0), stop=(kt == 7),
                         reads=[wjR] + rdm,
                         writes=[psR[pb]], inc=(kt == 7),
                         tag=L + f"qkv{j}_{tb}")

            def chain_slots(i):
                p3 = i % 3
                return dict(
                    RAW=f32(OFF_AUX + 4096 * p3, 512),
                    SH=f32(OFF_AUX + 4096 * p3 + 2048, 512),
                    QSQ=b16(OFF_SC + 2048 * p3, 512),
                    RQ=f32(OFF_SC + 2048 * p3, 512),
                    rawR=C.Ra(L + f"raw{p3}", OFF_AUX + 4096 * p3, 2048),
                    shR=C.Ra(L + f"sh{p3}", OFF_AUX + 4096 * p3 + 2048, 2048),
                    qsqR=C.Ra(L + f"qsq{p3}", OFF_SC + 2048 * p3, 2048))

            def qkv_chainA(i):
                tb, qk, pl = iters[i]
                j = 4 * qk + pl
                pb = i % 3
                s = chain_slots(i)
                RAW, SH, QSQ = s["RAW"], s["SH"], s["QSQ"]
                rawR, shR, qsqR = s["rawR"], s["shR"], s["qsqR"]
                C.act(QSQ, bank(pb), AF.Square,
                      reads=[psR[pb]], writes=[qsqR], tag=L + f"sq{j}{tb}")
                C.act(RAW, bank(pb), AF.Identity,
                      reads=[psR[pb]], writes=[rawR], tag=L + f"raw{j}{tb}")
                pq = 4 + (i % 3)
                C.mm(bank(pq), bones[:, :], QSQ, start=True, stop=True,
                     reads=[qsqR, R("bones")], writes=[psR[pq]],
                     inc=True, tag=L + f"ss{j}{tb}")
                for (d0, s0) in ((0, 32), (32, 0), (64, 96), (96, 64)):
                    C.dma("sync", SH[d0:d0 + 32, :], RAW[s0:s0 + 32, :],
                          reads=[rawR], writes=[shR], tag=L + f"shf{j}{tb}")

            def qkv_chainB(i):
                tb, qk, pl = iters[i]
                j = 4 * qk + pl
                s = chain_slots(i)
                RAW, SH, RQ = s["RAW"], s["SH"], s["RQ"]
                rawR, shR, qsqR = s["rawR"], s["shR"], s["qsqR"]
                pq = 4 + (i % 3)
                C.vop(lambda e, pq=pq, RQ=RQ: e.tensor_scalar(
                    RQ, bank(pq), 1e-24, None, ALU.max),
                    reads=[psR[pq]], writes=[qsqR], tag=L + f"ssm{j}{tb}")
                C.act(RQ, RQ, AF.Ln, reads=[qsqR], writes=[qsqR],
                      tag=L + f"ssln{j}{tb}")
                C.act(RQ, RQ, AF.Exp, scale=fc(-0.5),
                      reads=[qsqR, R("fconst")], writes=[qsqR],
                      tag=L + f"ssex{j}{tb}")
                tcq = TBL[:, 4096 * qk + 512 * tb:4096 * qk + 512 * tb + 512]
                tsq = TBL[:, 4096 * qk + 2048 + 512 * tb:
                          4096 * qk + 2048 + 512 * tb + 512]
                dst = (qa_pair(pl) if qk == 0 else ka_pair(pl))
                dstR = (qaC if qk == 0 else kaC)[pl][tb]
                p4 = i % 4
                M1 = f32(OFF_PT + 2048 * p4, 512)
                m1R = C.Ra(L + f"m1_{p4}", OFF_PT + 2048 * p4, 2048)
                C.vop(lambda e, tcq=tcq, RAW=RAW, M1=M1: e.tensor_mul(
                    M1, RAW, tcq),
                    reads=[rawR, tblR], writes=[m1R], tag=L + f"ro1{j}{tb}")
                C.vop(lambda e, tsq=tsq, SH=SH: e.scalar_tensor_tensor(
                    SH, SH, 1.0, tsq, ALU.mult, ALU.mult),
                    reads=[shR, tblR], writes=[shR], tag=L + f"ro2{j}{tb}")
                C.vop(lambda e, M1=M1, SH=SH: e.tensor_add(M1, M1, SH),
                      reads=[shR], writes=[m1R], tag=L + f"ro3{j}{tb}")
                C.vop(lambda e, dst=dst, tb=tb, M1=M1, RQ=RQ: e.tensor_mul(
                    dst[:, 512 * tb:512 * (tb + 1)], M1, RQ),
                    reads=[m1R, qsqR], writes=[dstR],
                    tag=L + f"ro4{j}{tb}")

            for i in range(2):
                qkv_mm(i)
                qkv_chainA(i)
            for i in range(len(iters)):
                if i + 2 < len(iters):
                    qkv_mm(i + 2)
                    qkv_chainA(i + 2)
                qkv_chainB(i)
            vfR = C.Ra(L + "vf", OFF_BIG, 24576)
            vones2 = b16(OFF_BIG, 12288).rearrange(
                "p (a c) -> p a c", c=192)[:, :, 64:128]
            C.vop(lambda e, v=vones2: e.memset(v, 1.0), writes=[vfR],
                  tag=L + "vones2")

            def vf_lhsT(kt, pl, odd):
                return b16(OFF_BIG + 2 * (kt * 768 + pl * 192 + (64 if odd else 0)), 128)

            for jv in range(4):
                slot = next_slot()
                wsl = stream_w_tile(fqkv_k, 1024 + 128 * jv, slot, L + f"wv{jv}")
                for tt in range(16):
                    pb = 3 if tt % 2 == 0 else 7
                    for kt in range(8):
                        C.mm(bank(pb)[:, 0:128],
                             b16(OFF_ACT + 2 * (2049 * kt + 128 * tt +
                                                (0 if kt >= 4 else 1)), 128),
                             wsl[:, 128 * kt:128 * (kt + 1)],
                             start=(kt == 0), stop=(kt == 7),
                             reads=[wrRa(slot), zmirTR[tt // 4]] +
                                   ([zmirTR[tt // 4 - 1]] if tt >= 4 else []),
                             writes=[psR[pb]], inc=(kt == 7),
                             tag=L + f"v{jv}{tt}")
                    dstv2 = b16(OFF_BIG + 2 * (tt * 768 + jv * 192),
                                256).rearrange(
                        "p (a c) -> p a c", c=128)[:, :, 0:64]
                    C.vop(lambda e, dstv2=dstv2, pb=pb: e.tensor_copy(
                        dstv2, bank(pb)[:, 0:128].rearrange(
                            "p (a c) -> p a c", c=64)),
                        reads=[psR[pb]], writes=[vfR], tag=L + f"vev{jv}{tt}")

            woWR = [C.Ra(L + f"woW{j}", OFF_ACT + 2048 * j, 2048)
                    for j in range(8)]
            for j in range(8):
                wsl = b16(OFF_ACT + 2048 * j, 512)
                C.dma("gpsimd", wsl.rearrange("p (k c) -> p k c", k=4),
                      fwo_k[:, :, 128 * j:128 * (j + 1)].transpose([1, 0, 2]),
                      writes=[woWR[j]] + zmirTR, tag=L + f"wow{j}")

            denR4 = [C.Ra(L + f"den{pl}", OFF_AUX + 2048 * pl, 2048)
                     for pl in range(4)]
            dtmR4 = [C.Ra(L + f"dtm{pl}", OFF_AUX + 8192 + 1024 * pl, 1024)
                     for pl in range(4)]
            fillers = []

            def pop_filler():
                _, _, fn = fillers.pop(0)
                fn()

            def drain_den(pl):
                while any(k == "den" and p == pl for k, p, _ in fillers):
                    pop_filler()

            def wo_quantum(j, tb):
                pb = j % 2
                wsl = b16(OFF_ACT + 2048 * j, 512)
                for kt in range(4):
                    C.mm(bank(pb), wsl[:, 128 * kt:128 * (kt + 1)],
                         qa_pair(kt)[:, 512 * tb:512 * (tb + 1)],
                         start=(kt == 0), stop=(kt == 3),
                         reads=[woWR[j], qaC[kt][tb]], writes=[psR[pb]],
                         inc=(kt == 3), tag=L + f"wo{j}{tb}")
                zt = zt_f32(j, tb)
                C.vop(lambda e, zt=zt, pb=pb: e.tensor_add(zt, bank(pb), zt),
                      reads=[psR[pb], zR[j]], writes=[zR[j]],
                      tag=L + f"wores{j}{tb}")

            for qb in range(NTB):
                ktlo = max(0, 4 * qb - 4)
                kthi = 4 * qb + 3
                kts = [4 * qb] + [t for t in range(ktlo, kthi + 1)
                                  if t != 4 * qb]
                ns = len(kts)
                for pl in range(4):
                    for step in range(ns + 1):
                        if step < ns:
                            kt = kts[step]
                            b0 = 2 if step % 2 == 0 else 4
                            for odd in range(2):
                                r0 = 64 * odd
                                C.mm(bank(b0 + odd),
                                     ka_pair(pl)[r0:r0 + 64,
                                                 128 * kt:128 * (kt + 1)],
                                     qa_pair(pl)[r0:r0 + 64,
                                                 512 * qb:512 * (qb + 1)],
                                     start=True, stop=True,
                                     reads=[qaC[pl][qb], kaC[pl][kt // 4]],
                                     writes=[psR[b0 + odd]],
                                     inc=(odd == 1), tile_position=(r0, 0),
                                     tag=L + f"st{pl}_{qb}_{kt}")
                        if fillers and step % 2 == 1:
                            pop_filler()
                        if step > 0:
                            kt = kts[step - 1]
                            d0 = 128 * kt - 512 * qb
                            lo = max(0, d0)
                            hi = min(512, d0 + 640)
                            b0 = 2 if (step - 1) % 2 == 0 else 4
                            p4a = (step - 1) % 4
                            ptb = OFF_PT + 2048 * p4a
                            ptuR = C.Ra(f"ptc{p4a}", ptb, 2048)
                            ptap = b16(ptb, 1024)
                            src3 = ps[:, 512 * b0:512 * b0 + 1024].rearrange(
                                "p (a c) -> p a c", c=512)[:, :, lo:hi]
                            dst3 = b16(ptb, 1024).rearrange(
                                "p (a c) -> p a c", c=512)[:, :, lo:hi]
                            C.act(dst3, src3, AF.Exp,
                                  reads=[psR[b0], psR[b0 + 1]],
                                  writes=[ptuR], tag=L + f"exp{pl}{qb}{kt}")
                            mi = 0 if d0 >= 0 else 1
                            f0 = d0 if d0 >= 0 else d0 + 512
                            for odd in range(2):
                                sl = ptap[:, 512 * odd + f0:512 * odd + f0 + 128]
                                C.vop(lambda e, sl=sl, mi=mi: e.tensor_mul(
                                    sl, sl, mk[:, 128 * mi:128 * mi + 128]),
                                    reads=[R("mk")], writes=[ptuR],
                                    tag=L + f"msk{pl}{qb}{kt}")
                            for odd in range(2):
                                C.mm(ps[:, 512 * (6 + odd) + lo:
                                        512 * (6 + odd) + hi],
                                     vf_lhsT(kt, pl, odd),
                                     b16(ptb + 2 * (512 * odd + lo), hi - lo),
                                     start=(kt == kts[0]), stop=(kt == kts[-1]),
                                     reads=[ptuR, vfR],
                                     writes=[psR[6 + odd]],
                                     inc=(kt == kts[-1]),
                                     tag=L + f"pv{pl}_{qb}_{kt}")
                    drain_den(pl)
                    DEN = f32(OFF_AUX + 2048 * pl, 512)
                    TMP = f32(OFF_AUX + 8192 + 1024 * pl, 512)
                    ot = qa_pair(pl)
                    denR = denR4[pl]
                    dtmR = dtmR4[pl]
                    C.vop(lambda e, ot=ot, qb=qb: e.tensor_copy(
                        ot[0:64, 512 * qb:512 * (qb + 1)], bank(6)[0:64, :]),
                        reads=[psR[6]], writes=[qaC[pl][qb]],
                        tag=L + f"oev{pl}{qb}")
                    C.vop(lambda e, ot=ot, qb=qb: e.tensor_copy(
                        ot[64:128, 512 * qb:512 * (qb + 1)],
                        bank(7)[64:128, :]),
                        reads=[psR[7]], writes=[qaC[pl][qb]],
                        tag=L + f"oev2{pl}{qb}")
                    C.vop(lambda e, DEN=DEN: e.tensor_copy(
                        DEN[64:128, :], bank(6)[64:128, :]),
                        reads=[psR[6]], writes=[denR], tag=L + f"dev{pl}{qb}")
                    C.vop(lambda e, DEN=DEN: e.tensor_copy(
                        DEN[0:64, :], bank(7)[0:64, :]),
                        reads=[psR[7]], writes=[denR], tag=L + f"dev2{pl}{qb}")
                    C.dma("sync", TMP[0:64, :], DEN[0:64, :],
                          reads=[denR], writes=[dtmR], tag=L + f"dsw{pl}{qb}")
                    C.dma("sync", DEN[0:64, :], DEN[64:128, :],
                          reads=[denR, dtmR], writes=[denR],
                          tag=L + f"dsw2{pl}{qb}")
                    C.dma("sync", DEN[64:128, :], TMP[0:64, :],
                          reads=[dtmR], writes=[denR], tag=L + f"dsw3{pl}{qb}")

                    def den_norm(pl=pl, qb=qb, DEN=DEN, denR=denR, ot=ot):
                        C.act(DEN, DEN, AF.Ln, reads=[denR], writes=[denR],
                              tag=L + f"dln{pl}{qb}")
                        C.act(DEN, DEN, AF.Exp, scale=fc(-1.0),
                              reads=[denR, R("fconst")], writes=[denR],
                              tag=L + f"dex{pl}{qb}")
                        C.vop(lambda e, ot=ot, qb=qb, DEN=DEN: e.tensor_mul(
                            ot[:, 512 * qb:512 * (qb + 1)],
                            ot[:, 512 * qb:512 * (qb + 1)], DEN),
                            reads=[denR], writes=[qaC[pl][qb]],
                            tag=L + f"on{pl}{qb}")

                    fillers.append(("den", pl, den_norm))
                for j in range(8):
                    fillers.append(
                        ("wo", None, lambda j=j, tb=qb: wo_quantum(j, tb)))
            while fillers:
                pop_filler()
            plan.flush()
            dump_fm(f"fo{lf}", lambda pl: (qa_pair(pl), qaC[pl]), nk=4)
            plan.flush()
            dump_fm(f"z{lf}a", lambda k: (zv[:, k:k + 1, 1:T + 1], [zR[k]]))
            hlnR = [[C.Ra(L + f"hln{k}_{tb}",
                          ((OFF_QA + 4096 * k) if k < 4 else
                           (OFF_KA + 4096 * (k - 4))) + 1024 * tb, 1024)
                     for tb in range(NTB)] for k in range(8)]
            lnxbR = [C.Ra(L + "lnxb0", OFF_BIG, 8192),
                     C.Ra(L + "lnxb1", OFF_BIG + 8192, 8192)]
            lnxqR = C.Ra(L + "lnxq", OFF_BIG + 16384, 8192)
            XBF = [b16(OFF_BIG, 4096), b16(OFF_BIG + 8192, 4096)]
            XQF = b16(OFF_BIG + 16384, 4096)

            def hln_ap(k, tb):
                base = (OFF_QA + 4096 * k) if k < 4 else (OFF_KA + 4096 * (k - 4))
                return b16(base + 1024 * tb, 512)

            def ffln_prefetch(tb):
                for grp in range(2):
                    ks, ke = 4 * grp, 4 * grp + 4
                    C.vop(lambda e, tb=tb, grp=grp, ks=ks, ke=ke: e.tensor_copy(
                        XBF[tb % 2][:, 2048 * grp:2048 * (grp + 1)]
                        .rearrange("p (k t) -> p k t", k=4),
                        zv[:, ks:ke,
                           512 * tb + (1 if grp == 0 else 0):
                           512 * tb + (1 if grp == 0 else 0) + 512]),
                        reads=[zR[k2] for k2 in range(ks, ke)],
                        writes=[lnxbR[tb % 2]], tag=L + f"lnld{tb}")

            layernorm(
                f32_src=lambda k, tb: (zt_f32(k, tb, shifted=True), [zR[k]]),
                bf_src=lambda tb: (XBF[tb % 2], [lnxbR[tb % 2]]),
                xq_ap=XQF, xqR=lnxqR,
                scale_col=lambda k: smcol(f"ln_g{lf}", k),
                bias_col=lambda k: smcol(f"ln_b{lf}", k),
                out_dst=lambda k, tb: (hln_ap(k, tb), [hlnR[k][tb]], []),
                tag=L + "ln", prefetch=ffln_prefetch)
            plan.flush()
            dump_fm(f"hln{lf}", lambda k: (
                b16(((OFF_QA + 4096 * k) if k < 4 else
                     (OFF_KA + 4096 * (k - 4))), 2048), hlnR[k]))

            gatedR = C.Ra(L + "gated", OFF_BIG, 32768)
            GELB = b16(OFF_SC, 512)
            mlist = list(range(22))
            chunks = [mlist[0:8], mlist[8:16], mlist[16:22]]
            for ci, msub in enumerate(chunks):
                for mi, m in enumerate(msub):
                    wd = 128 if m < 21 else 42
                    ca0 = 256 * m if m < 21 else 5376
                    sa = next_slot()
                    wa = stream_w_tile(fw1_k, ca0, sa, L + f"w1a{m}", width=wd)
                    sg = next_slot()
                    wg = stream_w_tile(fw1_k, ca0 + wd, sg, L + f"w1g{m}",
                                       width=wd)
                    for tb in range(NTB):
                        pw = ((4 * mi + tb) % 2) * 2
                        for kt in range(8):
                            C.mm(bank(pw)[0:wd, :], wa[:, wd * kt:wd * (kt + 1)],
                                 hln_ap(kt, tb), start=(kt == 0), stop=(kt == 7),
                                 reads=[wrRa(sa), hlnR[kt][tb]], writes=[psR[pw]],
                                 inc=(kt == 7), tag=L + f"w1a{m}{tb}")
                            C.mm(bank(pw + 1)[0:wd, :], wg[:, wd * kt:wd * (kt + 1)],
                                 hln_ap(kt, tb), start=(kt == 0), stop=(kt == 7),
                                 reads=[wrRa(sg), hlnR[kt][tb]], writes=[psR[pw + 1]],
                                 inc=(kt == 7), tag=L + f"w1g{m}{tb}")
                        par = (4 * mi + tb) % 2
                        GELc = b16(OFF_SC + 1024 * par, 512)
                        gelR = C.Ra(L + f"gel{par}", OFF_SC + 1024 * par, 1024)
                        C.act(GELc[0:wd, :], bank(pw + 1)[0:wd, :], AF.Gelu,
                              bias=smcol(f"b1{lf}", 2 * m + 1)[0:wd, :],
                              reads=[psR[pw + 1], R("sm")], writes=[gelR],
                              tag=L + f"gel{m}{tb}")
                        gt = b16(OFF_BIG + 2 * (2048 * mi + 512 * tb), 512)
                        C.vop(lambda e, gt=gt, wd=wd, m=m, pw=pw, GELc=GELc:
                              e.scalar_tensor_tensor(
                                  gt[0:wd, :], bank(pw)[0:wd, :],
                                  smcol(f"b1{lf}", 2 * m)[0:wd, :], GELc[0:wd, :],
                                  ALU.add, ALU.mult),
                              reads=[psR[pw], R("sm"), gelR],
                              writes=[gatedR], tag=L + f"gate{m}{tb}")
                for j in range(8):
                    slot = next_slot()
                    wsl = wr_slot(slot)
                    nfull = sum(1 for m in msub if m < 21)
                    if nfull:
                        C.dma("gpsimd",
                              wsl[:, 0:nfull * 128].rearrange(
                                  "p (k c) -> p k c", k=nfull),
                              fw2_full[msub[0]:msub[0] + nfull, :,
                                       128 * j:128 * (j + 1)].transpose([1, 0, 2]),
                              writes=[wrRa(slot)], tag=L + f"w2{ci}{j}")
                    if len(msub) > nfull:
                        C.dma("gpsimd",
                              wsl[0:42, nfull * 128:nfull * 128 + 128],
                              fw2_tail[:, 128 * j:128 * (j + 1)],
                              writes=[wrRa(slot)], tag=L + f"w2t{j}")
                    for tb in range(NTB):
                        pb = 4 + ((2 * j + tb) % 4)
                        for mi, m in enumerate(msub):
                            wd = 128 if m < 21 else 42
                            gt = b16(OFF_BIG + 2 * (2048 * mi + 512 * tb), 512)
                            C.mm(bank(pb), wsl[0:wd, 128 * mi:128 * (mi + 1)],
                                 gt[0:wd, :],
                                 start=(mi == 0), stop=(mi == len(msub) - 1),
                                 reads=[wrRa(slot), gatedR], writes=[psR[pb]],
                                 inc=(mi == len(msub) - 1),
                                 tag=L + f"w2{ci}{j}{tb}")
                        zt = zt_f32(j, tb)
                        if ci < 2:
                            C.vop(lambda e, zt=zt, pb=pb: e.tensor_add(
                                zt, bank(pb), zt),
                                reads=[psR[pb], zR[j]], writes=[zR[j]],
                                tag=L + f"w2res{ci}{j}{tb}")
                        else:
                            C.vop(lambda e, zt=zt, pb=pb, j=j:
                                  e.scalar_tensor_tensor(
                                      zt, bank(pb), smcol(f"b2{lf}", j), zt,
                                      ALU.add, ALU.add),
                                  reads=[psR[pb], R("sm"), zR[j]],
                                  writes=[zR[j]], tag=L + f"w2res{ci}{j}{tb}")
            plan.flush()
            dump_fm(f"z{lf}", lambda k: (zv[:, k:k + 1, 1:T + 1], [zR[k]]))
            if lf + 1 < DEPTH_F:
                for tb in range(NTB):
                    C.vop(lambda e, tb=tb: e.tensor_copy(
                        zbv[:, :, 1 + TBS * tb:1 + TBS * (tb + 1)],
                        zv[:, :, 1 + TBS * tb:1 + TBS * (tb + 1)]),
                        reads=[zR[k2] for k2 in range(8)],
                        writes=[C.zbTR[tb]], tag=L + f"zmir{tb}")
            plan.flush()


        if dev_phase < 7:
            early_exit()
            return nc

        XFp = [f32(OFF_AUX, 4096), f32(OFF_QA, 4096)]
        OUTSp = [f32(OFF_BIG, 4096), f32(OFF_BIG + 16384, 4096)]
        xfRf = [C.Ra("fin_xf0", OFF_AUX, 16384),
                C.Ra("fin_xf1", OFF_QA, 16384)]
        outR = [C.Ra("fin_out0", OFF_BIG, 16384),
                C.Ra("fin_out1", OFF_BIG + 16384, 16384)]
        youtR = C.R("yout")
        for tb in range(NTB):
            XF = XFp[tb % 2]
            OUTS = OUTSp[tb % 2]
            C.dma("sync", XF.rearrange("p (k t) -> p k t", k=8),
                  xs_k[:, :, TBS * tb:TBS * (tb + 1)].transpose([1, 0, 2]),
                  reads=[xscR], writes=[xfRf[tb % 2]], tag=f"fldx{tb}")
            for j in range(8):
                C.vop(lambda e, j=j, tb=tb, XF=XF, OUTS=OUTS:
                      e.scalar_tensor_tensor(
                          OUTS[:, 512 * j:512 * (j + 1)], zt_f32(j, tb),
                          modv[:, 72 + j:73 + j], XF[:, 512 * j:512 * (j + 1)],
                          ALU.mult, ALU.add),
                      reads=[zR[j], R("modd"), xfRf[tb % 2]],
                      writes=[outR[tb % 2]],
                      tag=f"fin{j}{tb}")
            C.dma("gpsimd",
                  y_k[:, :, TBS * tb:TBS * (tb + 1)].transpose([1, 0, 2]),
                  OUTS.rearrange("p (k t) -> p k t", k=8),
                  reads=[outR[tb % 2]], writes=[youtR], tag=f"fst{tb}")
        plan.flush()

        def final_wait(e):
            for name in list(plan.sems):
                if plan.cnt.get(name, 0) > 0:
                    e.wait_ge(plan.sems[name], plan.cnt[name])

        plan.pending["sync"].append(final_wait)
        plan.flush()

    return nc



def make_in_maps(inputs, host):
    x = np.asarray(inputs["L0_attn_x"], np.float32)
    c = np.asarray(inputs["L0_c"], np.float32)
    maps = []
    for b in range(NCORES):
        maps.append(dict(
            xt=np.ascontiguousarray(x[b].T),
            wqt=host["wqt"], wot=host["wot"], fqkv=host["fqkv"],
            fwo=host["fwo"], fw1=host["fw1"], fw2=host["fw2"],
            wmod=host["wmod"],
            cvec=np.ascontiguousarray(c[b].reshape(8, 128).T),
            small=host["small"], rtab=host["rtab"], masks=host["masks"],
        ))
    return maps


def kernel(**inputs):
    host = _host_prep(inputs)
    nc = build_program()
    in_maps = make_in_maps(inputs, host)
    res = run_bass_kernel_spmd(nc, in_maps, core_ids=list(range(NCORES)))
    out = np.stack([np.asarray(r["y"], np.float32).T for r in res.results])
    return np.ascontiguousarray(out)
